# version 29
# baseline (speedup 1.0000x reference)
import numpy as np
import ml_dtypes
from contextlib import ExitStack

import concourse.bass as bass
import concourse.mybir as mybir
from concourse.bass_utils import run_bass_kernel_spmd

F32 = mybir.dt.float32
F32R = mybir.dt.float32r
BF16 = mybir.dt.bfloat16
I32 = mybir.dt.int32
AF = mybir.ActivationFunctionType
ALU = mybir.AluOpType
AX = mybir.AxisListType

E = 1024
T = 2048
NSEQ = 2
EC = 8
TC = 4
NJC = 16
TT = 512
NTILE = NSEQ * T // TT
EPS = 1e-5
NEG = -1.0e30

PL_N = 40960
O_XTBF = 0
O_QT = 8192
O_KT = 16384
O_VV = 24576
O_WBFA = 32768
O_WST = 36864
O_WBFB = 24576
O_BVBC = 39936
O_MASK = 35840
O_RR = 8192
O_XSTR0 = 35840
O_XSTR1 = 37888
O_SQ0 = 35840
O_SQ1 = 37888
O_E2B = 35840
O_M2B = 37888
O_QI = 24576
O_SCL = 27648
O_TMP = 30720
O_LAST = 33792
O_LAR = 34816
O_LBST = 35328
O_LBR = 36352
O_WSTG = 36864
O_X1 = 24576
O_Y1 = 28672
O_H = 32768
O_HBF = 36864
O_YBF = 38912
O_OSB = 36864

PP_OFF = []
_lo, _hi = 0, 32768
for _jc in range(NJC):
    _wf = (2048 - 512 * (_jc // 4)) // 2
    if _jc < 8:
        PP_OFF.append(_lo)
        _lo += _wf
    else:
        PP_OFF.append(_hi)
        _hi += _wf
assert _lo <= 8192 and _hi <= O_MASK, (_lo, _hi)


def build_program():
    nc = bass.Bass("TRN2", target_bir_lowering=False)

    xt = nc.dram_tensor("xt", [NSEQ, E, T], F32, kind="ExternalInput").ap()
    wq_t = nc.dram_tensor("wq_t", [E, E], F32, kind="ExternalInput").ap()
    wk_t = nc.dram_tensor("wk_t", [E, E], F32, kind="ExternalInput").ap()
    wv_t = nc.dram_tensor("wv_t", [E, E], F32, kind="ExternalInput").ap()
    bq_h = nc.dram_tensor("bq_h", [128, 8], F32, kind="ExternalInput").ap()
    bk_h = nc.dram_tensor("bk_h", [128, 8], F32, kind="ExternalInput").ap()
    bv_h = nc.dram_tensor("bv_h", [128, E], F32, kind="ExternalInput").ap()
    qw_t = nc.dram_tensor("qw_t", [6, E, E], I32, kind="ExternalInput").ap()
    sc_t = nc.dram_tensor("sc_t", [6, E, E], F32, kind="ExternalInput").ap()
    la_h = nc.dram_tensor("la_h", [6, 32, E], F32, kind="ExternalInput").ap()
    lb_h = nc.dram_tensor("lb_h", [6, 32, E], F32, kind="ExternalInput").ap()
    fb_h = nc.dram_tensor("fb_h", [128, 48], F32, kind="ExternalInput").ap()
    gam_h = nc.dram_tensor("gam_h", [128, 24], F32, kind="ExternalInput").ap()
    bet_h = nc.dram_tensor("bet_h", [128, 24], F32, kind="ExternalInput").ap()
    mask_h = nc.dram_tensor("mask_h", [128, 2048], F32, kind="ExternalInput").ap()
    ones_h = nc.dram_tensor("ones_h", [128, 128], F32, kind="ExternalInput").ap()
    eye_h = nc.dram_tensor("eye_h", [128, 128], F32, kind="ExternalInput").ap()
    out_h = nc.dram_tensor("out", [NSEQ, T, E], F32, kind="ExternalOutput").ap()
    x1s = nc.dram_tensor("x1s", [NSEQ, EC, 128, T], F32)
    weffs = nc.dram_tensor("weffs", [6, EC, 128, E], BF16)

    with ExitStack() as ctx:
        sbt = lambda n, s, d: ctx.enter_context(nc.sbuf_tensor(n, s, d))
        PL = sbt("PL", [128, PL_N], F32)
        ones_st = sbt("ones_st", [128, 128], F32)
        ones_b = sbt("ones_b", [128, 128], BF16)
        eye = sbt("eye", [128, 128], F32)
        bq_t = sbt("bq_t", [128, 8], F32)
        bk_t = sbt("bk_t", [128, 8], F32)
        fb = sbt("fb", [128, 48], F32)
        gam = sbt("gam", [128, 24], F32)
        bet = sbt("bet", [128, 24], F32)
        muB = sbt("muB", [128, 2048], F32)
        rsB = sbt("rsB", [128, 2048], F32)
        e2s = sbt("e2s", [128, 1024], F32)
        m2s = sbt("m2s", [128, 1024], F32)
        sqd = sbt("sqd", [128, 2048], F32)
        Zt = sbt("Zt", [128, 16], F32)
        RZ = sbt("RZ", [128, 16], F32)
        Mx = sbt("Mx", [128, 8], F32)
        ps = [ctx.enter_context(nc.psum_tensor(f"ps{i}", [128, 512], F32))
              for i in range(8)]

        plf = PL[:]
        plb = PL[:].bitcast(BF16)
        plr = PL[:].bitcast(F32R)
        pli = PL[:].bitcast(I32)
        fv = lambda a, n: plf[:, a:a + n]
        bfv = lambda a, nbf: plb[:, 2 * a:2 * a + nbf]
        rv = lambda a, n: plr[:, a:a + n]
        iv = lambda a, n: pli[:, a:a + n]
        sqr = sqd[:].bitcast(F32R)

        def ppseg(jc):
            return bfv(PP_OFF[jc], 2048 - 512 * (jc // 4))

        sem = {}
        cnt = {}
        for name in ("lc", "lx0", "lx1", "lw", "lw0", "lw1", "lf0", "lf1", "gl",
                     "gq0", "gq1", "gq2", "sw0", "sw1", "gp", "sp", "sl0", "sl1",
                     "pe", "ve", "se"):
            sem[name] = ctx.enter_context(nc.semaphore(name))
            cnt[name] = 0

        def inc(inst, name, n=1):
            cnt[name] += n
            inst.then_inc(sem[name], n)
            return cnt[name]

        def wt(e, name, v):
            e.wait_ge(sem[name], v)

        def chain(e, name):
            e.wait_ge(sem[name], cnt[name])

        blk = ctx.enter_context(nc.Block())
        _scope_stack = []

        def scope(name):
            sid, _ = nc.enter_named_scope(name, False)
            _scope_stack.append((name, sid))

        def escope():
            name, sid = _scope_stack.pop()
            nc.leave_named_scope(name, sid, False)

        def barrier():
            @blk.tensor
            def _(e):
                inc(e.drain(), "pe")

            @blk.vector
            def _(e):
                inc(e.drain(), "ve")

            @blk.scalar
            def _(e):
                inc(e.drain(), "se")
            marks = dict(cnt)
            for secname in ("tensor", "vector", "scalar", "sync", "gpsimd"):
                def mk(e, m=marks):
                    for n, v in m.items():
                        e.wait_ge(sem[n], v)
                getattr(blk, secname)(mk)

        scope("consts")
        @blk.sync
        def _(e):
            for dst, src in [(ones_st[:], ones_h), (eye[:], eye_h),
                             (bq_t[:], bq_h), (bk_t[:], bk_h), (fb[:], fb_h),
                             (gam[:], gam_h), (bet[:], bet_h)]:
                inc(e.dma_start(out=dst, in_=src), "lc", 16)
        _ld0 = cnt["lc"]

        @blk.vector
        def _(e):
            wt(e, "lc", _ld0)
            inc(e.tensor_copy(ones_b[:], ones_st[:]), "ve")

        barrier()

        escope()
        def load_w(wsrc, area):
            vb = cnt["ve"]
            lb = [cnt["lw0"], cnt["lw1"]]

            @blk.sync
            def _(e):
                for c in range(EC):
                    wt(e, "ve", vb if c < 2 else vb + c - 1)
                    inc(e.dma_start(out=fv(O_WST + (c % 2) * 1024, 1024),
                                    in_=wsrc[c * 128:(c + 1) * 128, :]),
                        "lw%d" % (c % 2), 16)

            @blk.vector
            def _(e):
                for c in range(EC):
                    wt(e, "lw%d" % (c % 2), lb[c % 2] + 16 * (c // 2 + 1))
                    inc(e.tensor_copy(bfv(area + c * 512, 1024),
                                      fv(O_WST + (c % 2) * 1024, 1024)), "ve")
            assert cnt["ve"] == vb + EC
            return cnt["ve"]

        def load_w_s(wsrc, area, extra_first=()):
            lb = [cnt["lw0"], cnt["lw1"]]
            seB = cnt["se"]

            @blk.scalar
            def _(e):
                for c in range(EC):
                    wt(e, "lw%d" % (c % 2), lb[c % 2] + 16 * (c // 2 + 1))
                    inc(e.activation(bfv(area + c * 512, 1024),
                                     fv(O_WST + (c % 2) * 1024, 1024),
                                     AF.Copy), "se")
            assert cnt["se"] == seB + EC

            @blk.sync
            def _(e):
                for nm2, v2 in extra_first:
                    wt(e, nm2, v2)
                for c in range(EC):
                    if c >= 2:
                        wt(e, "se", seB + c - 1)
                    inc(e.dma_start(out=fv(O_WST + (c % 2) * 1024, 1024),
                                    in_=wsrc[c * 128:(c + 1) * 128, :]),
                        "lw%d" % (c % 2), 16)
            return ("se", seB + EC)

        sqm = {}
        sp_marks = []
        for seq in range(NSEQ):
            xtbf = bfv(O_XTBF, 16384)
            xst = [fv(O_WBFA, 2048), fv(O_WBFA + 2048, 2048)]
            f1_prev = sqm.get("f1", 0)
            scope(f"xqkv{seq}")

            if seq == 0:
                ve0 = cnt["ve"]
                lxb = [cnt["lx0"], cnt["lx1"]]

                @blk.sync
                def _(e):
                    for c in range(EC):
                        if c >= 2:
                            wt(e, "ve", ve0 + c - 1)
                        inc(e.dma_start(out=xst[c % 2],
                                        in_=xt[seq, c * 128:(c + 1) * 128, :]),
                            "lx%d" % (c % 2), 16)

                @blk.vector
                def _(e):
                    for c in range(EC):
                        wt(e, "lx%d" % (c % 2), lxb[c % 2] + 16 * (c // 2 + 1))
                        inc(e.tensor_copy(bfv(O_XTBF + c * 1024, 2048), xst[c % 2]), "ve")
                assert cnt["ve"] == ve0 + EC
                x_cast_done = cnt["ve"]

                wq_mark = ("ve", load_w(wq_t, O_WBFA))
                wk_mark = ("ve", load_w(wk_t, O_WBFB))
            else:
                wq_mark = load_w_s(wq_t, O_WBFA,
                                   extra_first=(sqm["xpf"],
                                                ("pe", sqm["stats_pe"])))
                wk_mark = load_w_s(wk_t, O_WBFB,
                                   extra_first=(("se", wq_mark[1]),))

            def qkv_mms(area, dst_off, bias_tile, w_mark, bank_free=None,
                        rr_spill=None):
                pe_b = cnt["pe"]
                se_b = cnt["se"]

                @blk.tensor
                def _(e):
                    wt(e, w_mark[0], w_mark[1])
                    wt(e, "se", se_b if bank_free is None else bank_free)
                    g = 0
                    for s in range(EC):
                        for tc in range(TC):
                            bank = g % 2
                            if g >= 2:
                                wt(e, "se", se_b + g - 1)
                            for c in range(EC):
                                last = e.matmul(
                                    ps[bank][:],
                                    bfv(area + c * 512, 1024)[:, s * 128:(s + 1) * 128],
                                    xtbf[:, c * 2048 + tc * 512:c * 2048 + (tc + 1) * 512],
                                    start=(c == 0), stop=(c == EC - 1))
                            inc(last, "pe")
                            g += 1

                @blk.scalar
                def _(e):
                    g = 0
                    if rr_spill is not None:
                        wt(e, rr_spill[0], rr_spill[1])
                    for s in range(EC):
                        for tc in range(TC):
                            bank = g % 2
                            wt(e, "pe", pe_b + g + 1)
                            inc(e.add(
                                bfv(dst_off + s * 1024 + tc * 256, 512),
                                ps[bank][:], bias_tile[:, s:s + 1]), "se")
                            g += 1
                assert cnt["pe"] == pe_b + 32 and cnt["se"] == se_b + 32

            rr_sp0 = sp_marks[seq - 1] if seq else None
            qkv_mms(O_WBFA, O_QT, bq_t, wq_mark,
                    bank_free=(f1_prev if seq else None), rr_spill=rr_sp0)
            q_pe_done = cnt["pe"]
            qkv_mms(O_WBFB, O_KT, bk_t, wk_mark, rr_spill=rr_sp0)

            @blk.sync
            def _(e):
                wt(e, "pe", q_pe_done)

            if seq == 0:
                @blk.vector
                def _(e):
                    wt(e, "pe", q_pe_done)
                wv_mark = ("ve", load_w(wv_t, O_WBFA))
            else:
                wv_mark = load_w_s(wv_t, O_WBFA,
                                   extra_first=(("se", wk_mark[1]),))

            bvb = cnt["lc"]

            @blk.sync
            def _(e):
                inc(e.dma_start(out=fv(O_BVBC, 1024), in_=bv_h), "lc", 16)

            pe_bv = cnt["pe"]
            ve_bv = cnt["ve"]

            @blk.tensor
            def _(e):
                if f1_prev:
                    wt(e, "se", f1_prev)
                wt(e, wv_mark[0], wv_mark[1])
                g = 0
                for ts in range(16):
                    for hh in range(2):
                        bank = 2 + g % 2
                        if g >= 2:
                            wt(e, "ve", ve_bv + g - 1)
                        for c in range(EC):
                            last = e.matmul(
                                ps[bank][:],
                                xtbf[:, c * 2048 + ts * 128:c * 2048 + ts * 128 + 128],
                                bfv(O_WBFA + c * 512, 1024)[:, hh * 512:(hh + 1) * 512],
                                start=(c == 0), stop=(c == EC - 1))
                        inc(last, "pe")
                        g += 1

            @blk.vector
            def _(e):
                wt(e, "lc", bvb + 16)
                g = 0
                for ts in range(16):
                    for hh in range(2):
                        bank = 2 + g % 2
                        wt(e, "pe", pe_bv + g + 1)
                        inc(e.tensor_add(
                            bfv(O_VV + ts * 512 + hh * 256, 512),
                            ps[bank][:], fv(O_BVBC + hh * 512, 512)), "ve")
                        g += 1
            assert cnt["ve"] == ve_bv + 32

            escope()
            scope(f"scores{seq}")
            barrier()

            mld = cnt["lc"]

            @blk.sync
            def _(e):
                inc(e.dma_start(out=fv(O_MASK, 2048), in_=mask_h), "lc", 16)

            qt = bfv(O_QT, 16384)
            kt = bfv(O_KT, 16384)
            pe_b = cnt["pe"]
            ve_b = cnt["ve"]
            se_b = cnt["se"]
            pe_m, ve_m, se_m = {}, {}, {}
            pc, vc, sc_ = pe_b, ve_b, se_b
            for jc in range(NJC):
                nv = 4 - jc // 4
                pc += 1
                pe_m[jc] = pc
                vc += 2
                ve_m[jc] = vc
                sc_ += nv
                se_m[jc] = sc_

            @blk.tensor
            def _(e):
                if f1_prev:
                    wt(e, "se", f1_prev)
                for jc in range(NJC):
                    ic0 = jc // 4
                    base = (jc % 2) * 4
                    if jc >= 2:
                        wt(e, "se", se_m[jc - 2])
                    for ic in range(ic0, 4):
                        for c in range(EC):
                            last = e.matmul(
                                ps[base + ic][:],
                                kt[:, c * 2048 + jc * 128:c * 2048 + jc * 128 + 128],
                                qt[:, c * 2048 + ic * 512:c * 2048 + (ic + 1) * 512],
                                start=(c == 0), stop=(c == EC - 1))
                    inc(last, "pe")
            assert cnt["pe"] == pe_m[NJC - 1]

            @blk.vector
            def _(e):
                wt(e, "lc", mld + 16)
                for jc in range(NJC):
                    ic0 = jc // 4
                    base = (jc % 2) * 4
                    wt(e, "pe", pe_m[jc])
                    r = jc % 4
                    inc(e.tensor_add(ps[base + ic0][:], ps[base + ic0][:],
                                     fv(O_MASK + r * 512, 512)), "ve")
                    wt(e, "se", se_m[jc])
                    inc(e.reduce_sum(Zt[:, jc:jc + 1], ppseg(jc), axis=AX.X), "ve")
            assert cnt["ve"] == ve_m[NJC - 1]

            @blk.scalar
            def _(e):
                for jc in range(NJC):
                    ic0 = jc // 4
                    base = (jc % 2) * 4
                    wt(e, "pe", pe_m[jc])
                    for k_ in range(4 - ic0):
                        if k_ == 0:
                            wt(e, "ve", ve_m[jc] - 1)
                        inc(e.activation(
                            ppseg(jc)[:, k_ * 512:(k_ + 1) * 512],
                            ps[base + ic0 + k_][:], AF.Exp,
                            bias=0.0, scale=1.0 / 32.0), "se")
            assert cnt["se"] == se_m[NJC - 1]

            @blk.vector
            def _(e):
                chain(e, "ve")
                inc(e.reciprocal(RZ[:], Zt[:]), "ve")
                chain(e, "ve")
                for ts in range(16):
                    inc(e.tensor_scalar_mul(bfv(O_VV + ts * 512, 1024),
                                            bfv(O_VV + ts * 512, 1024),
                                            RZ[:, ts:ts + 1]), "ve")

            escope()
            scope(f"phase2_{seq}")
            barrier()

            pe_b2 = cnt["pe"]
            ve_b2 = cnt["ve"]
            xb2 = [cnt["lx0"], cnt["lx1"]]
            p2_pe = {es: pe_b2 + es + 1 for es in range(EC)}
            p2_ve = {es: ve_b2 + 4 * (es + 1) for es in range(EC)}
            xstr = [fv(O_XSTR0, 2048), fv(O_XSTR1, 2048)]

            @blk.sync
            def _(e):
                for es in range(EC):
                    if es >= 2:
                        wt(e, "ve", p2_ve[es - 2])
                    inc(e.dma_start(out=xstr[es % 2],
                                    in_=xt[seq, es * 128:(es + 1) * 128, :]),
                        "lx%d" % (es % 2), 16)

            @blk.tensor
            def _(e):
                for es in range(EC):
                    base = (es % 2) * 4
                    if es >= 2:
                        wt(e, "ve", p2_ve[es - 2])
                    for jc in range(NJC):
                        ic0 = jc // 4
                        for ic in range(ic0, 4):
                            last = e.matmul(
                                ps[base + ic][:],
                                bfv(O_VV + jc * 512, 1024)[:, es * 128:es * 128 + 128],
                                ppseg(jc)[:, (ic - ic0) * 512:(ic - ic0 + 1) * 512],
                                start=(jc == 0), stop=(jc == 4 * ic + 3))
                    inc(last, "pe")

            @blk.vector
            def _(e):
                if seq:
                    wt(e, *sp_marks[seq - 1])
                for es in range(EC):
                    base = (es % 2) * 4
                    wt(e, "pe", p2_pe[es])
                    wt(e, "lx%d" % (es % 2), xb2[es % 2] + 16 * (es // 2 + 1))
                    for ic in range(4):
                        inc(e.tensor_add(fv(O_RR + es * 2048 + ic * 512, 512),
                                         ps[base + ic][:],
                                         xstr[es % 2][:, ic * 512:(ic + 1) * 512]), "ve")
            assert cnt["ve"] == ve_b2 + 32

            escope()
            scope(f"ln1_{seq}")
            barrier()

            sqb = [O_SQ0, O_SQ1]
            se_b3 = cnt["se"]
            pe_b3 = cnt["pe"]
            ve_b3 = cnt["ve"]

            @blk.scalar
            def _(e):
                for c in range(EC):
                    if c >= 2:
                        wt(e, "pe", pe_b3 + c - 1)
                    inc(e.activation(bfv(sqb[c % 2] + 1024, 2048),
                                     fv(O_RR + c * 2048, 2048), AF.Square), "se")

            @blk.vector
            def _(e):
                for c in range(EC):
                    if c >= 2:
                        wt(e, "pe", pe_b3 + c - 1)
                    inc(e.tensor_copy(bfv(sqb[c % 2], 2048),
                                      fv(O_RR + c * 2048, 2048)), "ve")

            @blk.tensor
            def _(e):
                for c in range(EC):
                    wt(e, "se", se_b3 + c + 1)
                    wt(e, "ve", ve_b3 + c + 1)
                    for tc in range(TC):
                        e.matmul(ps[tc][:], ones_b[:],
                                 bfv(sqb[c % 2], 2048)[:, tc * 512:(tc + 1) * 512],
                                 start=(c == 0), stop=(c == EC - 1))
                        last = e.matmul(ps[4 + tc][:], ones_b[:],
                                 bfv(sqb[c % 2] + 1024, 2048)[:, tc * 512:(tc + 1) * 512],
                                 start=(c == 0), stop=(c == EC - 1))
                    inc(last, "pe")
            sqm["stats_pe"] = cnt["pe"]
            sqm["p2_pe"] = p2_pe[EC - 1]

            if seq + 1 < NSEQ:
                pf_base = cnt["ve"]
                pfl = [cnt["lx0"], cnt["lx1"]]
                pf_st = [fv(O_VV, 2048), fv(O_VV + 2048, 2048)]
                p2e = p2_pe[EC - 1]

                @blk.sync
                def _(e):
                    wt(e, "pe", p2e)
                    for c in range(EC):
                        if c >= 2:
                            wt(e, "ve", pf_base + c - 1)
                        inc(e.dma_start(out=pf_st[c % 2],
                                        in_=xt[seq + 1, c * 128:(c + 1) * 128, :]),
                            "lx%d" % (c % 2), 16)

                @blk.vector
                def _(e):
                    wt(e, "pe", p2e)
                    for c in range(EC):
                        wt(e, "lx%d" % (c % 2), pfl[c % 2] + 16 * (c // 2 + 1))
                        inc(e.tensor_copy(bfv(O_XTBF + c * 1024, 2048),
                                          pf_st[c % 2]), "ve")
                assert cnt["ve"] == pf_base + EC
                sqm["xpf"] = ("ve", pf_base + EC)

            barrier()

            e2o = 28672 if seq + 1 < NSEQ else O_E2B
            m2o = 30720 if seq + 1 < NSEQ else O_M2B

            @blk.scalar
            def _(e):
                for tc in range(TC):
                    inc(e.activation(muB[:, tc * 512:(tc + 1) * 512], ps[tc][:],
                                     AF.Copy, scale=1.0), "se")
                    inc(e.activation(fv(e2o + tc * 512, 512), ps[4 + tc][:],
                                     AF.Copy, bias=EPS, scale=1.0), "se")
                chain(e, "se")
                inc(e.activation(fv(m2o, 2048), muB[:, 0:2048], AF.Square), "se")
            f1 = cnt["se"]
            sqm["f1"] = f1

            @blk.vector
            def _(e):
                wt(e, "se", f1)
                inc(e.tensor_sub(fv(e2o, 2048), fv(e2o, 2048), fv(m2o, 2048)), "ve")
            f2 = cnt["ve"]

            @blk.scalar
            def _(e):
                wt(e, "ve", f2)
                inc(e.activation(fv(m2o, 2048), fv(e2o, 2048), AF.Ln), "se")
                chain(e, "se")
                inc(e.activation(rsB[:, 0:2048], fv(m2o, 2048), AF.Exp, scale=-0.5), "se")
            f3 = cnt["se"]
            sqm["f3"] = f3

            ve_b4 = cnt["ve"]
            gp_b4 = cnt["gp"]
            sqm["ve_apply_base"] = ve_b4

            @blk.vector
            def _(e):
                wt(e, "se", f3)
                for c in range(5):
                    inc(e.tensor_sub(fv(O_RR + c * 2048, 2048),
                                     fv(O_RR + c * 2048, 2048), muB[:, 0:2048]), "ve")
                    chain(e, "ve")
                    inc(e.tensor_mul(fv(O_RR + c * 2048, 2048),
                                     fv(O_RR + c * 2048, 2048), rsB[:, 0:2048]), "ve")
                    inc(e.drain(), "ve")

            @blk.gpsimd
            def _(e):
                wt(e, "se", f3)
                for c in range(5, EC):
                    inc(e.tensor_sub(fv(O_RR + c * 2048, 2048),
                                     fv(O_RR + c * 2048, 2048), muB[:, 0:2048]), "gp")
                    chain(e, "gp")
                    inc(e.tensor_mul(fv(O_RR + c * 2048, 2048),
                                     fv(O_RR + c * 2048, 2048), rsB[:, 0:2048]), "gp")
                    inc(e.drain(), "gp")
                for c in range(EC):
                    if c < 5:
                        wt(e, "ve", ve_b4 + 3 * (c + 1))
                    else:
                        wt(e, "gp", gp_b4 + 3 * (c - 5 + 1))
                    inc(e.dma_start(out=x1s.ap()[seq, c, :, :],
                                    in_=fv(O_RR + c * 2048, 2048)),
                        "sl%d" % seq, 16)
            sp_marks.append(("sl%d" % seq, cnt["sl%d" % seq]))

            escope()

        attn_end = dict(cnt)
        w_sw_marks = []
        _dqst = {"prev_dqve7": sqm["ve_apply_base"], "prev_lacse": 0,
                 "prev_lora_pe": 0}

        def dequant_w(w):
            prev_dqve = _dqst["prev_dqve7"]
            la_ld = cnt["gl"]

            @blk.sync
            def _(e):
                wt(e, "pe", sqm["p2_pe"])
                wt(e, "se", max(sqm["f3"], _dqst["prev_lacse"]))
                inc(e.dma_start(out=fv(O_LAST, 1024)[0:32, :], in_=la_h[w]), "gl", 16)
                inc(e.dma_start(out=fv(O_LBST, 1024)[0:32, :], in_=lb_h[w]), "gl", 16)

            @blk.scalar
            def _(e):
                wt(e, "gl", la_ld + 32)
                if _dqst["prev_lora_pe"]:
                    wt(e, "pe", _dqst["prev_lora_pe"])
                e.activation(bfv(O_LAR, 1024)[0:32, :],
                             fv(O_LAST, 1024)[0:32, :], AF.Copy)
                inc(e.activation(bfv(O_LBR, 1024)[0:32, :],
                                 fv(O_LBST, 1024)[0:32, :], AF.Copy), "se")
            lac_se = cnt["se"]
            _dqst["prev_lacse"] = lac_se

            qldb = [cnt["gq0"], cnt["gq1"], cnt["gq2"]]
            swb = [cnt["sw0"], cnt["sw1"]]
            se_bq = cnt["se"]
            pe_bq = cnt["pe"]
            veB = cnt["ve"]
            dq_ve = {c: veB + 3 * (c + 1) for c in range(EC)}

            @blk.sync
            def _(e):
                wt(e, "pe", sqm["p2_pe"])
                for c in range(EC):
                    wt(e, "ve", dq_ve[c - 3] if c >= 3 else prev_dqve)
                    inc(e.dma_start(out=iv(O_QI + (c % 3) * 1024, 1024),
                                    in_=qw_t[w, c * 128:(c + 1) * 128, :]),
                        "gq%d" % (c % 3), 16)
                    inc(e.dma_start(out=fv(O_SCL + (c % 3) * 1024, 1024),
                                    in_=sc_t[w, c * 128:(c + 1) * 128, :]),
                        "gq%d" % (c % 3), 16)

            @blk.gpsimd
            def _(e):
                for c in range(EC):
                    wt(e, "ve", dq_ve[c])
                    inc(e.dma_start(out=weffs.ap()[w, c],
                                    in_=bfv(O_WSTG + (c % 2) * 512, 1024)),
                        "sw%d" % (c % 2), 16)

            @blk.scalar
            def _(e):
                for c in range(EC):
                    wt(e, "gq%d" % (c % 3), qldb[c % 3] + 32 * (c // 3 + 1))
                    wt(e, "ve", dq_ve[c - 3] if c >= 3 else prev_dqve)
                    inc(e.activation(fv(O_TMP + (c % 3) * 1024, 1024),
                                     iv(O_QI + (c % 3) * 1024, 1024),
                                     AF.Copy, bias=-1.0, scale=1.0 / 7.5), "se")

            @blk.tensor
            def _(e):
                wt(e, "se", lac_se)
                wt(e, "ve", prev_dqve)
                for c in range(EC):
                    bank = 4 + 2 * (c % 2)
                    if c >= 2:
                        wt(e, "ve", dq_ve[c - 2])
                    e.matmul(ps[bank][:], bfv(O_LAR, 1024)[0:32, c * 128:(c + 1) * 128],
                             bfv(O_LBR, 1024)[0:32, 0:512], start=True, stop=True)
                    last = e.matmul(ps[bank + 1][:], bfv(O_LAR, 1024)[0:32, c * 128:(c + 1) * 128],
                             bfv(O_LBR, 1024)[0:32, 512:1024], start=True, stop=True)
                    inc(last, "pe")

            @blk.vector
            def _(e):
                for c in range(EC):
                    bank = 4 + 2 * (c % 2)
                    wt(e, "se", se_bq + c + 1)
                    wt(e, "gq%d" % (c % 3), qldb[c % 3] + 32 * (c // 3 + 1))
                    wt(e, "pe", pe_bq + c + 1)
                    wt(e, "sw%d" % (c % 2), swb[c % 2] + 16 * (c // 2))
                    inc(e.tensor_mul(fv(O_TMP + (c % 3) * 1024, 1024),
                                     fv(O_TMP + (c % 3) * 1024, 1024),
                                     fv(O_SCL + (c % 3) * 1024, 1024)), "ve")
                    chain(e, "ve")
                    e.tensor_add(bfv(O_WSTG + (c % 2) * 512, 1024)[:, 0:512],
                                 fv(O_TMP + (c % 3) * 1024, 512), ps[bank][:])
                    inc(e.tensor_add(bfv(O_WSTG + (c % 2) * 512, 1024)[:, 512:1024],
                                     fv(O_TMP + (c % 3) * 1024 + 512, 512),
                                     ps[bank + 1][:]), "ve")
                    inc(e.drain(), "ve")
            assert cnt["ve"] == dq_ve[EC - 1]
            _dqst["prev_dqve7"] = dq_ve[EC - 1]
            _dqst["prev_lora_pe"] = pe_bq + EC

            w_sw_marks.append((cnt["sw0"], cnt["sw1"]))
        WSL = [0, 4096]
        IBASE = [8192, 8192 + 14336]
        sqdb4 = sqd[:].bitcast(BF16)
        inst_free_pe = [attn_end["pe"], attn_end["pe"]]
        wslot_pe = [attn_end["pe"], attn_end["pe"]]
        stat_read = [("se", attn_end["se"]), ("se", attn_end["se"])]
        osb_sp = cnt["sp"]
        YB = [0, 1, 4, 5]
        ygrp = {"n": 0, "guard": {0: ("se", attn_end["se"]),
                                  1: ("se", attn_end["se"]),
                                  4: ("se", attn_end["se"]),
                                  5: ("se", attn_end["se"])}}
        tr_idx = {"n": 0}

        def bc512(tile_ap_tensor, col_off):
            return bass.AP(tile_ap_tensor, col_off, [[2048, 128], [0, 8], [1, 512]])

        def w_load(j):
            sl = j % 2
            nm = "lw%d" % sl
            free = wslot_pe[sl]
            swm = w_sw_marks[j]

            @blk.sync
            def _(e):
                wt(e, "pe", free)
                wt(e, "sw0", swm[0])
                wt(e, "sw1", swm[1])
                inc(e.dma_start(
                    out=bfv(WSL[sl], 8192).rearrange("p (c n) -> p c n", n=1024),
                    in_=weffs.ap()[j].rearrange("c p n -> p c n")), nm, 16)
            return (nm, cnt[nm])

        def x1_load(t, extra=()):
            i = t % 2
            seq_, off_ = t // (T // TT), (t % (T // TT)) * TT
            nm = "lf%d" % i
            free = inst_free_pe[i]

            @blk.sync
            def _(e):
                wt(e, "pe", free)
                for m in sp_marks:
                    wt(e, *m)
                for nm2, v2 in extra:
                    wt(e, nm2, v2)
                for c in range(EC):
                    inc(e.dma_start(out=fv(IBASE[i] + c * 512, 512),
                                    in_=x1s.ap()[seq_, c, :, off_:off_ + TT]), nm, 16)
            return (nm, cnt[nm])

        def cast_in(t, src_off, x1mark=None, extra=()):
            i = t % 2

            @blk.vector
            def _(e):
                if x1mark is not None:
                    wt(e, x1mark[0], x1mark[1])
                for nm2, v2 in extra:
                    wt(e, nm2, v2)
                chain(e, "ve")
                inc(e.tensor_copy(bfv(IBASE[i] + 12288, 4096), fv(src_off, 4096)), "ve")
            return ("ve", cnt["ve"])

        def y_mms(t, wj, wmark, castmark, evac_kind, evac_args):
            i = t % 2
            sl = wj % 2
            ioff = i * 2048
            pe_b_ = cnt["pe"]
            se_b_ = cnt["se"]
            ve_b_ = cnt["ve"]
            gp_b_ = cnt["gp"]
            guards = ygrp["guard"]
            g0 = ygrp["n"]

            b0, b1 = (2, 3) if i == 0 else (6, 7)
            rdmark = stat_read[i]

            def stat2(e, s_):
                if evac_kind == "y1":
                    src_sum = bfv(IBASE[i] + 4096, 4096)[:, s_ * 512:(s_ + 1) * 512]
                else:
                    src_sum = sqdb4[:, ioff + (s_ % 2) * 1024:ioff + (s_ % 2) * 1024 + 512]
                e.matmul(ps[b0][:], ones_b[:], src_sum,
                         start=(s_ == 0), stop=(s_ == EC - 1))
                return e.matmul(ps[b1][:], ones_b[:],
                                sqdb4[:, ioff + (s_ % 2) * 1024 + 512:ioff + (s_ % 2) * 1024 + 1024],
                                start=(s_ == 0), stop=(s_ == EC - 1))

            @blk.tensor
            def _(e):
                wt(e, castmark[0], castmark[1])
                wt(e, wmark[0], wmark[1])
                if evac_kind == "y1":
                    wt(e, rdmark[0], rdmark[1])
                for s in range(EC):
                    bank = YB[(g0 + s) % 4]
                    gg = guards.get(bank)
                    if gg is not None:
                        wt(e, gg[0], gg[1])
                    for c in range(EC):
                        src_mv = (bfv(IBASE[i] + 12288, 4096) if evac_kind == "y1"
                                  else bfv(IBASE[i] + 6144, 4096))
                        last = e.matmul(
                            ps[bank][:],
                            bfv(WSL[sl], 8192)[:, c * 1024 + s * 128:c * 1024 + (s + 1) * 128],
                            src_mv[:, c * 512:(c + 1) * 512],
                            start=(c == 0), stop=(c == EC - 1))
                    if evac_kind == "y1" and s >= 1:
                        wt(e, "se", se_b_ + 2 * s)
                        last = stat2(e, s - 1)
                    inc(last, "pe")
                    guards[bank] = (("se" if evac_kind == "y1" else "ve"),
                                    (se_b_ + 2 * s + 1) if evac_kind == "y1"
                                    else (ve_b_ + 2 * s + 1))
                if evac_kind == "y1":
                    wt(e, "se", se_b_ + 2 * EC)
                    inc(stat2(e, EC - 1), "pe")

            if evac_kind == "y1":
                ybf = bfv(IBASE[i] + 4096, 4096)

                @blk.scalar
                def _(e):
                    for s in range(EC):
                        wt(e, "pe", pe_b_ + s + 1)
                        inc(e.add(ybf[:, s * 512:(s + 1) * 512],
                                  ps[YB[(g0 + s) % 4]][:], evac_args[:, s:s + 1]), "se")
                        chain(e, "se")
                        inc(e.activation(
                            sqdb4[:, ioff + (s % 2) * 1024 + 512:ioff + (s % 2) * 1024 + 1024],
                            ybf[:, s * 512:(s + 1) * 512], AF.Square), "se")
            else:
                fbcol, in1_off, tail = evac_args

                @blk.vector
                def _(e):
                    for s in range(EC):
                        wt(e, "pe", pe_b_ + s + 1)
                        inc(e.scalar_tensor_tensor(
                            fv(IBASE[i] + 8192 + s * 512, 512),
                            ps[YB[(g0 + s) % 4]][:], fbcol[:, s:s + 1],
                            fv(in1_off + s * 512, 512), ALU.add, ALU.add), "ve")
                        chain(e, "ve")
                        if tail == "cast":
                            inc(e.tensor_copy(
                                bfv(IBASE[i] + 12288, 4096)[:, s * 512:(s + 1) * 512],
                                fv(IBASE[i] + 8192 + s * 512, 512)), "ve")
                        else:
                            inc(e.tensor_add(
                                fv(IBASE[i] + 8192 + s * 512, 512),
                                fv(IBASE[i] + 8192 + s * 512, 512),
                                fv(IBASE[i] + s * 512, 512)), "ve")
            ygrp["n"] += EC
            wslot_pe[sl] = cnt["pe"]
            return ("ve", cnt["ve"])

        def stats_mms(t, se_b_, ve_b_, sq_guard=True):
            i = t % 2
            b0, b1 = (2, 3) if i == 0 else (6, 7)
            ioff = i * 2048
            rdmark = stat_read_se[i]

            @blk.tensor
            def _(e):
                wt(e, "se", rdmark)
                for s_ in range(EC):
                    wt(e, "se", se_b_ + 2 * s_ + 2)
                    wt(e, "ve", ve_b_ + s_ + 1)
                    e.matmul(ps[b0][:], ones_b[:],
                             sqdb4[:, ioff + (s_ % 2) * 1024:ioff + (s_ % 2) * 1024 + 512],
                             start=(s_ == 0), stop=(s_ == EC - 1))
                    last = e.matmul(ps[b1][:], ones_b[:],
                             sqdb4[:, ioff + (s_ % 2) * 1024 + 512:ioff + (s_ % 2) * 1024 + 1024],
                             start=(s_ == 0), stop=(s_ == EC - 1))
                    inc(last, "pe")
            return cnt["pe"]

        def ln_stats_r2(t, src_off, wait_ve):
            i = t % 2
            ioff = i * 2048
            se_b_ = cnt["se"]
            ve_b_ = cnt["ve"]
            pe_b_ = cnt["pe"]

            @blk.scalar
            def _(e):
                wt(e, "ve", wait_ve)
                chain(e, "se")
                for s_ in range(EC):
                    if s_ >= 2:
                        wt(e, "pe", pe_b_ + s_ - 1)
                    inc(e.activation(
                        sqdb4[:, ioff + (s_ % 2) * 1024 + 512:ioff + (s_ % 2) * 1024 + 1024],
                        fv(src_off + s_ * 512, 512), AF.Square), "se")

            @blk.vector
            def _(e):
                chain(e, "ve")
                for s_ in range(EC):
                    if s_ >= 2:
                        wt(e, "pe", pe_b_ + s_ - 1)
                    inc(e.tensor_copy(
                        sqdb4[:, ioff + (s_ % 2) * 1024:ioff + (s_ % 2) * 1024 + 512],
                        fv(src_off + s_ * 512, 512)), "ve")

            b0, b1 = ((2, 3) if t % 2 == 0 else (6, 7))
            rdmark = stat_read[t % 2]

            @blk.tensor
            def _(e):
                wt(e, rdmark[0], rdmark[1])
                for s_ in range(EC):
                    wt(e, "se", se_b_ + s_ + 1)
                    wt(e, "ve", ve_b_ + s_ + 1)
                    e.matmul(ps[b0][:], ones_b[:],
                             sqdb4[:, ioff + (s_ % 2) * 1024:ioff + (s_ % 2) * 1024 + 512],
                             start=(s_ == 0), stop=(s_ == EC - 1))
                    last = e.matmul(ps[b1][:], ones_b[:],
                             sqdb4[:, ioff + (s_ % 2) * 1024 + 512:ioff + (s_ % 2) * 1024 + 1024],
                             start=(s_ == 0), stop=(s_ == EC - 1))
                    inc(last, "pe")
            return cnt["pe"]

        def ln_fin2(t, mu_copy=False):
            i = t % 2
            b0, b1 = (2, 3) if i == 0 else (6, 7)
            mcol = muB[:, i * 512:(i + 1) * 512]
            rcol = rsB[:, i * 512:(i + 1) * 512]
            ecol = e2s[:, i * 512:(i + 1) * 512]
            m2col = m2s[:, i * 512:(i + 1) * 512]
            p = cnt["pe"]

            @blk.scalar
            def _(e):
                wt(e, "pe", p)
                if mu_copy:
                    inc(e.copy(mcol, ps[b0][:]), "se")
                inc(e.activation(m2col, ps[b0][:], AF.Square), "se")
                inc(e.activation(ecol, ps[b1][:], AF.Copy, bias=EPS), "se")
            if mu_copy:
                stat_read[i] = ("se", cnt["se"])
            g1 = cnt["se"]

            @blk.vector
            def _(e):
                wt(e, "se", g1)
                inc(e.tensor_sub(ecol, ecol, m2col), "ve")
            g2 = cnt["ve"]

            @blk.scalar
            def _(e):
                wt(e, "ve", g2)
                inc(e.activation(m2col, ecol, AF.Ln), "se")
                chain(e, "se")
                inc(e.activation(rcol, m2col, AF.Exp, scale=-0.5), "se")
            return cnt["se"]

        def apply_gelu(t, blki, fin):
            i = t % 2
            b0 = 2 if i == 0 else 6
            ybf = bfv(IBASE[i] + 4096, 4096)
            y3 = ybf.rearrange("p (x b) -> p x b", b=512)
            mub = bass.AP(ps[b0], 0, [[512, 128], [0, 8], [1, 512]])

            @blk.vector
            def _(e):
                wt(e, "se", fin)
                chain(e, "ve")
                inc(e.tensor_sub(y3, y3, mub), "ve")
                chain(e, "ve")
                inc(e.tensor_mul(y3, y3, bc512(rsB, i * 512)), "ve")
            stat_read[i] = ("ve", cnt["ve"] - 1)
            am = cnt["ve"]

            @blk.scalar
            def _(e):
                wt(e, "ve", am)
                for s in range(EC):
                    inc(e.activation(bfv(IBASE[i] + 6144, 4096)[:, s * 512:(s + 1) * 512],
                                     ybf[:, s * 512:(s + 1) * 512], AF.Gelu,
                                     scale=gam[:, blki * 8 + s:blki * 8 + s + 1],
                                     bias=bet[:, blki * 8 + s:blki * 8 + s + 1]), "se")
            return ("se", cnt["se"])

        def r2_add(t):
            i = t % 2
            ho = IBASE[i] + 8192

            @blk.vector
            def _(e):
                chain(e, "ve")
                inc(e.tensor_add(fv(ho, 4096), fv(ho, 4096), fv(IBASE[i], 4096)), "ve")
            return cnt["ve"]

        def ln2_apply(t, fin):
            i = t % 2
            ho = IBASE[i] + 8192
            xo = IBASE[i]
            h3 = fv(ho, 4096).rearrange("p (x b) -> p x b", b=512)
            x3 = fv(xo, 4096).rearrange("p (x b) -> p x b", b=512)
            ve_o = cnt["ve"]

            @blk.vector
            def _(e):
                wt(e, "se", fin)
                chain(e, "ve")
                inc(e.tensor_sub(x3, h3, bc512(muB, i * 512)), "ve")
                chain(e, "ve")
                inc(e.tensor_mul(x3, x3, bc512(rsB, i * 512)), "ve")
            return cnt["ve"]

        def trans_store(t, ve_o):
            i = t % 2
            seq_, off_ = t // (T // TT), (t % (T // TT)) * TT
            xo = IBASE[i]
            pe_t = cnt["pe"]
            se_t = cnt["se"]
            sp_prev = globals_osb[0]

            @blk.tensor
            def _(e):
                wt(e, "ve", ve_o)
                idx = 0
                for tsl in range(4):
                    for c in range(EC):
                        n = tr_idx["n"] + idx
                        bank = 4 + n % 2
                        gg = ygrp["guard"].get(bank)
                        if gg is not None:
                            wt(e, gg[0], gg[1])
                        inc(e.transpose(ps[bank][:, 0:128],
                                        fv(xo + c * 512 + tsl * 128, 128), eye[:]),
                            "pe")
                        ygrp["guard"][bank] = ("se", se_t + idx + 1 + idx // EC)
                        idx += 1

            @blk.scalar
            def _(e):
                wt(e, "sp", sp_prev)
                idx = 0
                for tsl in range(4):
                    for c in range(EC):
                        n = tr_idx["n"] + idx
                        wt(e, "pe", pe_t + idx + 1)
                        inc(e.copy(fv(36864 + tsl * 1024 + c * 128, 128),
                                   ps[4 + n % 2][:, 0:128]), "se")
                        idx += 1
                    inc(e.drain(), "se")

            @blk.gpsimd
            def _(e):
                for tsl in range(4):
                    wt(e, "se", se_t + (tsl + 1) * (EC + 1))
                    inc(e.dma_start(
                        out=out_h[seq_, off_ + tsl * 128:off_ + (tsl + 1) * 128, :],
                        in_=fv(36864 + tsl * 1024, 1024)), "sp", 16)
            tr_idx["n"] += 32
            inst_free_pe[i] = cnt["pe"]
            return cnt["sp"]

        fbcols = [fb[:, w * 8:(w + 1) * 8] for w in range(6)]

        def emit_group(gname, ts, ex_x1=None, ex_cast=None, hook=None,
                       pre=None):
            nonlocal_osb = None
            scope(gname)
            xm = dict((t, m[0]) for t, m in (pre or {}).items())
            wl = {}
            cm = dict((t, m[1]) for t, m in (pre or {}).items())
            for j in (0, 1):
                if hook:
                    hook(j)
            if ts[0] not in xm:
                xm[ts[0]] = x1_load(ts[0], (ex_x1 or {}).get(ts[0], ()))
            wl[0] = w_load(0)
            wl[1] = w_load(1)
            for t in ts[1:]:
                if t not in xm:
                    xm[t] = x1_load(t, (ex_x1 or {}).get(t, ()))
            for t in ts:
                if t not in cm:
                    cm[t] = cast_in(t, IBASE[t % 2], xm[t],
                                    extra=(ex_cast or {}).get(t, ()))
            for blki in range(3):
                w1, w2 = 2 * blki, 2 * blki + 1
                gel = {}
                fin = {}
                for t in ts:
                    y_mms(t, w1, wl[w1], cm[t], "y1", fbcols[w1])
                    fin[t] = ln_fin2(t)
                for t in ts:
                    gel[t] = apply_gelu(t, blki, fin[t])
                if w1 + 2 < 6:
                    if hook:
                        hook(w1 + 2)
                    wl[w1 + 2] = w_load(w1 + 2)
                for t in ts:
                    in1 = IBASE[t % 2] if blki == 0 else IBASE[t % 2] + 8192
                    tail = "cast" if blki < 2 else "r2"
                    cm[t] = y_mms(t, w2, wl[w2], gel[t], "y2",
                                  (fbcols[w2], in1, tail))
                if w2 + 2 < 6:
                    if hook:
                        hook(w2 + 2)
                    wl[w2 + 2] = w_load(w2 + 2)
            for t in ts:
                ln_stats_r2(t, IBASE[t % 2] + 8192, cm[t][1])
            vo = {}
            for t in ts:
                fin2 = ln_fin2(t, mu_copy=True)
                vo[t] = ln2_apply(t, fin2)
            for t in ts:
                globals_osb[0] = trans_store(t, vo[t])
            escope()

        globals_osb = [osb_sp]

        xm0 = x1_load(0, (("ve", attn_end["ve"]),))
        cm0 = cast_in(0, IBASE[0], xm0)

        scope("dequant")
        for w in range(6):
            dequant_w(w)
        escope()
        dq_end = dict(cnt)
        stat_read[1] = ("ve", dq_end["ve"])
        ygrp["guard"][4] = ("ve", dq_end["ve"])
        ygrp["guard"][5] = ("ve", dq_end["ve"])
        ex0 = {1: (("se", dq_end["se"]), ("ve", dq_end["ve"]),
                   ("gq0", dq_end["gq0"]), ("gq1", dq_end["gq1"]),
                   ("gq2", dq_end["gq2"]))}
        exc0 = {1: (("sw0", dq_end["sw0"]), ("sw1", dq_end["sw1"]))}
        emit_group("ffng0", (0, 1), ex_x1=ex0, ex_cast=exc0,
                   pre={0: (xm0, cm0)})
        emit_group("ffng1", (2, 3))
        emit_group("ffng2", (4, 5))
        emit_group("ffng3", (6, 7))

        barrier()

    return nc


_CACHE = {}


def host_shared(L0_attn_Wq, L0_attn_bq, L0_attn_Wk, L0_attn_bk, L0_attn_Wv,
                L0_attn_bv, L0_ffn_qweights, L0_ffn_scales, L0_ffn_biases,
                L0_ffn_lora_a, L0_ffn_lora_b, L0_ffn_ln_gamma, L0_ffn_ln_beta,
                **_ignored):
    wq_t = np.ascontiguousarray(np.asarray(L0_attn_Wq, np.float32).T)
    wk_t = np.ascontiguousarray(np.asarray(L0_attn_Wk, np.float32).T)
    wv_t = np.ascontiguousarray(np.asarray(L0_attn_Wv, np.float32).T)
    bq = np.asarray(L0_attn_bq, np.float32)
    bk = np.asarray(L0_attn_bk, np.float32)
    bv_ = np.asarray(L0_attn_bv, np.float32)
    bq_h = np.ascontiguousarray(bq.reshape(8, 128).T)
    bk_h = np.ascontiguousarray(bk.reshape(8, 128).T)
    bv_h = np.ascontiguousarray(np.broadcast_to(bv_[None, :], (128, E)))
    qw_t = np.ascontiguousarray(np.asarray(L0_ffn_qweights, np.int32).transpose(0, 2, 1))
    sc = np.asarray(L0_ffn_scales, np.float32)
    sc_t = np.ascontiguousarray(np.repeat(sc, 16, axis=2).transpose(0, 2, 1))
    la_h = np.ascontiguousarray(np.asarray(L0_ffn_lora_a, np.float32))
    lb_h = np.ascontiguousarray(np.asarray(L0_ffn_lora_b, np.float32).transpose(0, 2, 1))
    fb_h = np.ascontiguousarray(
        np.asarray(L0_ffn_biases, np.float32).reshape(6, 8, 128).transpose(2, 0, 1).reshape(128, 48))
    gam_h = np.ascontiguousarray(
        np.asarray(L0_ffn_ln_gamma, np.float32).reshape(3, 8, 128).transpose(2, 0, 1).reshape(128, 24))
    bet_h = np.ascontiguousarray(
        np.asarray(L0_ffn_ln_beta, np.float32).reshape(3, 8, 128).transpose(2, 0, 1).reshape(128, 24))
    p = np.arange(128)[:, None]
    c = np.arange(512)[None, :]
    mask_h = np.concatenate(
        [np.where(c >= p + 128 * r, 0.0, NEG).astype(np.float32) for r in range(4)],
        axis=1)
    ones_h = np.full((128, 128), 1.0 / 1024.0, np.float32)
    eye_h = np.eye(128, dtype=np.float32)

    return dict(wq_t=wq_t, wk_t=wk_t, wv_t=wv_t, bq_h=bq_h, bk_h=bk_h,
                bv_h=bv_h, qw_t=qw_t, sc_t=sc_t, la_h=la_h, lb_h=lb_h,
                fb_h=fb_h, gam_h=gam_h, bet_h=bet_h, mask_h=mask_h,
                ones_h=ones_h, eye_h=eye_h)


def kernel(**inputs):
    X = np.ascontiguousarray(np.asarray(inputs["L0_attn_X"], dtype=np.float32))
    shared = host_shared(**inputs)
    if "nc" not in _CACHE:
        _CACHE["nc"] = build_program()
    nc = _CACHE["nc"]
    in_maps = []
    for core in range(8):
        xtc = np.ascontiguousarray(X[2 * core:2 * core + 2].transpose(0, 2, 1))
        in_maps.append(dict(shared, xt=xtc))

    res = run_bass_kernel_spmd(nc, in_maps, core_ids=list(range(8)))
    out = np.concatenate([res.results[cr]["out"] for cr in range(8)], axis=0)
    return np.ascontiguousarray(out.astype(np.float32))


# version 30
# speedup vs baseline: 1.0216x; 1.0216x over previous
import numpy as np
import ml_dtypes
from contextlib import ExitStack

import concourse.bass as bass
import concourse.mybir as mybir
from concourse.bass_utils import run_bass_kernel_spmd

F32 = mybir.dt.float32
F32R = mybir.dt.float32r
BF16 = mybir.dt.bfloat16
I32 = mybir.dt.int32
AF = mybir.ActivationFunctionType
ALU = mybir.AluOpType
AX = mybir.AxisListType

E = 1024
T = 2048
NSEQ = 2
EC = 8
TC = 4
NJC = 16
TT = 512
NTILE = NSEQ * T // TT
EPS = 1e-5
NEG = -1.0e30

PL_N = 40960
O_XTBF = 0
O_QT = 8192
O_KT = 16384
O_VV = 24576
O_WBFA = 32768
O_WST = 36864
O_WBFB = 24576
O_BVBC = 39936
O_MASK = 35840
O_RR = 8192
O_XSTR0 = 35840
O_XSTR1 = 37888
O_SQ0 = 35840
O_SQ1 = 37888
O_E2B = 35840
O_M2B = 37888
O_QI = 24576
O_SCL = 27648
O_TMP = 30720
O_LAST = 33792
O_LAR = 34816
O_LBST = 35328
O_LBR = 36352
O_WSTG = 36864
O_X1 = 24576
O_Y1 = 28672
O_H = 32768
O_HBF = 36864
O_YBF = 38912
O_OSB = 36864

PP_OFF = []
_lo, _hi = 0, 32768
for _jc in range(NJC):
    _wf = (2048 - 512 * (_jc // 4)) // 2
    if _jc < 8:
        PP_OFF.append(_lo)
        _lo += _wf
    else:
        PP_OFF.append(_hi)
        _hi += _wf
assert _lo <= 8192 and _hi <= O_MASK, (_lo, _hi)


def build_program():
    nc = bass.Bass("TRN2", target_bir_lowering=False)

    xt = nc.dram_tensor("xt", [NSEQ, E, T], F32, kind="ExternalInput").ap()
    wq_t = nc.dram_tensor("wq_t", [E, E], F32, kind="ExternalInput").ap()
    wk_t = nc.dram_tensor("wk_t", [E, E], F32, kind="ExternalInput").ap()
    wv_t = nc.dram_tensor("wv_t", [E, E], F32, kind="ExternalInput").ap()
    bq_h = nc.dram_tensor("bq_h", [128, 8], F32, kind="ExternalInput").ap()
    bk_h = nc.dram_tensor("bk_h", [128, 8], F32, kind="ExternalInput").ap()
    bv_h = nc.dram_tensor("bv_h", [128, E], F32, kind="ExternalInput").ap()
    qw_t = nc.dram_tensor("qw_t", [6, E, E], I32, kind="ExternalInput").ap()
    sc_t = nc.dram_tensor("sc_t", [6, E, E], F32, kind="ExternalInput").ap()
    la_h = nc.dram_tensor("la_h", [6, 32, E], F32, kind="ExternalInput").ap()
    lb_h = nc.dram_tensor("lb_h", [6, 32, E], F32, kind="ExternalInput").ap()
    fb_h = nc.dram_tensor("fb_h", [128, 48], F32, kind="ExternalInput").ap()
    gam_h = nc.dram_tensor("gam_h", [128, 24], F32, kind="ExternalInput").ap()
    bet_h = nc.dram_tensor("bet_h", [128, 24], F32, kind="ExternalInput").ap()
    mask_h = nc.dram_tensor("mask_h", [128, 2048], F32, kind="ExternalInput").ap()
    ones_h = nc.dram_tensor("ones_h", [128, 128], F32, kind="ExternalInput").ap()
    eye_h = nc.dram_tensor("eye_h", [128, 128], F32, kind="ExternalInput").ap()
    out_h = nc.dram_tensor("out", [NSEQ, T, E], F32, kind="ExternalOutput").ap()
    x1s = nc.dram_tensor("x1s", [NSEQ, EC, 128, T], F32)
    weffs = nc.dram_tensor("weffs", [6, EC, 128, E], BF16)

    with ExitStack() as ctx:
        sbt = lambda n, s, d: ctx.enter_context(nc.sbuf_tensor(n, s, d))
        PL = sbt("PL", [128, PL_N], F32)
        ones_st = sbt("ones_st", [128, 128], F32)
        ones_b = sbt("ones_b", [128, 128], BF16)
        eye = sbt("eye", [128, 128], F32)
        bq_t = sbt("bq_t", [128, 8], F32)
        bk_t = sbt("bk_t", [128, 8], F32)
        fb = sbt("fb", [128, 48], F32)
        gam = sbt("gam", [128, 24], F32)
        bet = sbt("bet", [128, 24], F32)
        muB = sbt("muB", [128, 2048], F32)
        rsB = sbt("rsB", [128, 2048], F32)
        e2s = sbt("e2s", [128, 1024], F32)
        m2s = sbt("m2s", [128, 1024], F32)
        sqd = sbt("sqd", [128, 2048], F32)
        Zt = sbt("Zt", [128, 16], F32)
        RZ = sbt("RZ", [128, 16], F32)
        Mx = sbt("Mx", [128, 8], F32)
        ps = [ctx.enter_context(nc.psum_tensor(f"ps{i}", [128, 512], F32))
              for i in range(8)]

        plf = PL[:]
        plb = PL[:].bitcast(BF16)
        plr = PL[:].bitcast(F32R)
        pli = PL[:].bitcast(I32)
        fv = lambda a, n: plf[:, a:a + n]
        bfv = lambda a, nbf: plb[:, 2 * a:2 * a + nbf]
        rv = lambda a, n: plr[:, a:a + n]
        iv = lambda a, n: pli[:, a:a + n]
        sqr = sqd[:].bitcast(F32R)

        def ppseg(jc):
            return bfv(PP_OFF[jc], 2048 - 512 * (jc // 4))

        sem = {}
        cnt = {}
        for name in ("lc", "lx0", "lx1", "lw", "lw0", "lw1", "lf0", "lf1", "gl",
                     "gq0", "gq1", "gq2", "sw0", "sw1", "gp", "sp", "sl0", "sl1",
                     "pe", "ve", "se"):
            sem[name] = ctx.enter_context(nc.semaphore(name))
            cnt[name] = 0

        def inc(inst, name, n=1):
            cnt[name] += n
            inst.then_inc(sem[name], n)
            return cnt[name]

        def wt(e, name, v):
            e.wait_ge(sem[name], v)

        def chain(e, name):
            e.wait_ge(sem[name], cnt[name])

        blk = ctx.enter_context(nc.Block())
        _scope_stack = []

        def scope(name):
            sid, _ = nc.enter_named_scope(name, False)
            _scope_stack.append((name, sid))

        def escope():
            name, sid = _scope_stack.pop()
            nc.leave_named_scope(name, sid, False)

        def barrier():
            @blk.tensor
            def _(e):
                inc(e.drain(), "pe")

            @blk.vector
            def _(e):
                inc(e.drain(), "ve")

            @blk.scalar
            def _(e):
                inc(e.drain(), "se")
            marks = dict(cnt)
            for secname in ("tensor", "vector", "scalar", "sync", "gpsimd"):
                def mk(e, m=marks):
                    for n, v in m.items():
                        e.wait_ge(sem[n], v)
                getattr(blk, secname)(mk)

        scope("consts")
        @blk.sync
        def _(e):
            for dst, src in [(ones_st[:], ones_h), (eye[:], eye_h),
                             (bq_t[:], bq_h), (bk_t[:], bk_h), (fb[:], fb_h),
                             (gam[:], gam_h), (bet[:], bet_h)]:
                inc(e.dma_start(out=dst, in_=src), "lc", 16)
        _ld0 = cnt["lc"]

        @blk.vector
        def _(e):
            wt(e, "lc", _ld0)
            inc(e.tensor_copy(ones_b[:], ones_st[:]), "ve")

        barrier()

        escope()
        def load_w(wsrc, area):
            vb = cnt["ve"]
            lb = [cnt["lw0"], cnt["lw1"]]

            @blk.sync
            def _(e):
                for c in range(EC):
                    wt(e, "ve", vb if c < 2 else vb + c - 1)
                    inc(e.dma_start(out=fv(O_WST + (c % 2) * 1024, 1024),
                                    in_=wsrc[c * 128:(c + 1) * 128, :]),
                        "lw%d" % (c % 2), 16)

            @blk.vector
            def _(e):
                for c in range(EC):
                    wt(e, "lw%d" % (c % 2), lb[c % 2] + 16 * (c // 2 + 1))
                    inc(e.tensor_copy(bfv(area + c * 512, 1024),
                                      fv(O_WST + (c % 2) * 1024, 1024)), "ve")
            assert cnt["ve"] == vb + EC
            return cnt["ve"]

        def load_w_s(wsrc, area, extra_first=()):
            lb = [cnt["lw0"], cnt["lw1"]]
            seB = cnt["se"]

            @blk.scalar
            def _(e):
                for c in range(EC):
                    wt(e, "lw%d" % (c % 2), lb[c % 2] + 16 * (c // 2 + 1))
                    inc(e.activation(bfv(area + c * 512, 1024),
                                     fv(O_WST + (c % 2) * 1024, 1024),
                                     AF.Copy), "se")
            assert cnt["se"] == seB + EC

            @blk.sync
            def _(e):
                for nm2, v2 in extra_first:
                    wt(e, nm2, v2)
                for c in range(EC):
                    if c >= 2:
                        wt(e, "se", seB + c - 1)
                    inc(e.dma_start(out=fv(O_WST + (c % 2) * 1024, 1024),
                                    in_=wsrc[c * 128:(c + 1) * 128, :]),
                        "lw%d" % (c % 2), 16)
            return ("se", seB + EC)

        sqm = {}
        sp_marks = []
        for seq in range(NSEQ):
            xtbf = bfv(O_XTBF, 16384)
            xst = [fv(O_WBFA, 2048), fv(O_WBFA + 2048, 2048)]
            f1_prev = sqm.get("f1", 0)
            scope(f"xqkv{seq}")

            if seq == 0:
                ve0 = cnt["ve"]
                lxb = [cnt["lx0"], cnt["lx1"]]

                @blk.sync
                def _(e):
                    for c in range(EC):
                        if c >= 2:
                            wt(e, "ve", ve0 + c - 1)
                        inc(e.dma_start(out=xst[c % 2],
                                        in_=xt[seq, c * 128:(c + 1) * 128, :]),
                            "lx%d" % (c % 2), 16)

                @blk.vector
                def _(e):
                    for c in range(EC):
                        wt(e, "lx%d" % (c % 2), lxb[c % 2] + 16 * (c // 2 + 1))
                        inc(e.tensor_copy(bfv(O_XTBF + c * 1024, 2048), xst[c % 2]), "ve")
                assert cnt["ve"] == ve0 + EC
                x_cast_done = cnt["ve"]

                wq_mark = ("ve", load_w(wq_t, O_WBFA))
                wk_mark = ("ve", load_w(wk_t, O_WBFB))
            else:
                wq_mark = load_w_s(wq_t, O_WBFA,
                                   extra_first=(sqm["xpf"],
                                                ("pe", sqm["stats_pe"])))
                wk_mark = load_w_s(wk_t, O_WBFB,
                                   extra_first=(("se", wq_mark[1]),))

            def qkv_mms(area, dst_off, bias_tile, w_mark, bank_free=None,
                        rr_spill=None):
                pe_b = cnt["pe"]
                se_b = cnt["se"]

                @blk.tensor
                def _(e):
                    wt(e, w_mark[0], w_mark[1])
                    wt(e, "se", se_b if bank_free is None else bank_free)
                    g = 0
                    for s in range(EC):
                        for tc in range(TC):
                            bank = g % 2
                            if g >= 2:
                                wt(e, "se", se_b + g - 1)
                            for c in range(EC):
                                last = e.matmul(
                                    ps[bank][:],
                                    bfv(area + c * 512, 1024)[:, s * 128:(s + 1) * 128],
                                    xtbf[:, c * 2048 + tc * 512:c * 2048 + (tc + 1) * 512],
                                    start=(c == 0), stop=(c == EC - 1))
                            inc(last, "pe")
                            g += 1

                @blk.scalar
                def _(e):
                    g = 0
                    if rr_spill is not None:
                        wt(e, rr_spill[0], rr_spill[1])
                    for s in range(EC):
                        for tc in range(TC):
                            bank = g % 2
                            wt(e, "pe", pe_b + g + 1)
                            inc(e.add(
                                bfv(dst_off + s * 1024 + tc * 256, 512),
                                ps[bank][:], bias_tile[:, s:s + 1]), "se")
                            g += 1
                assert cnt["pe"] == pe_b + 32 and cnt["se"] == se_b + 32

            rr_sp0 = sp_marks[seq - 1] if seq else None
            qkv_mms(O_WBFA, O_QT, bq_t, wq_mark,
                    bank_free=(f1_prev if seq else None), rr_spill=rr_sp0)
            q_pe_done = cnt["pe"]
            qkv_mms(O_WBFB, O_KT, bk_t, wk_mark, rr_spill=rr_sp0)

            @blk.sync
            def _(e):
                wt(e, "pe", q_pe_done)

            if seq == 0:
                @blk.vector
                def _(e):
                    wt(e, "pe", q_pe_done)
                wv_mark = ("ve", load_w(wv_t, O_WBFA))
            else:
                wv_mark = load_w_s(wv_t, O_WBFA,
                                   extra_first=(("se", wk_mark[1]),))

            bvb = cnt["lc"]

            @blk.sync
            def _(e):
                inc(e.dma_start(out=fv(O_BVBC, 1024), in_=bv_h), "lc", 16)

            pe_bv = cnt["pe"]
            ve_bv = cnt["ve"]

            @blk.tensor
            def _(e):
                if f1_prev:
                    wt(e, "se", f1_prev)
                wt(e, wv_mark[0], wv_mark[1])
                g = 0
                for ts in range(16):
                    for hh in range(2):
                        bank = 2 + g % 2
                        if g >= 2:
                            wt(e, "ve", ve_bv + g - 1)
                        for c in range(EC):
                            last = e.matmul(
                                ps[bank][:],
                                xtbf[:, c * 2048 + ts * 128:c * 2048 + ts * 128 + 128],
                                bfv(O_WBFA + c * 512, 1024)[:, hh * 512:(hh + 1) * 512],
                                start=(c == 0), stop=(c == EC - 1))
                        inc(last, "pe")
                        g += 1

            @blk.vector
            def _(e):
                wt(e, "lc", bvb + 16)
                g = 0
                for ts in range(16):
                    for hh in range(2):
                        bank = 2 + g % 2
                        wt(e, "pe", pe_bv + g + 1)
                        inc(e.tensor_add(
                            bfv(O_VV + ts * 512 + hh * 256, 512),
                            ps[bank][:], fv(O_BVBC + hh * 512, 512)), "ve")
                        g += 1
            assert cnt["ve"] == ve_bv + 32

            escope()
            scope(f"scores{seq}")
            barrier()

            mld = cnt["lc"]

            @blk.sync
            def _(e):
                inc(e.dma_start(out=fv(O_MASK, 2048), in_=mask_h), "lc", 16)

            qt = bfv(O_QT, 16384)
            kt = bfv(O_KT, 16384)
            pe_b = cnt["pe"]
            ve_b = cnt["ve"]
            se_b = cnt["se"]
            pe_m, ve_m, se_m = {}, {}, {}
            pc, vc, sc_ = pe_b, ve_b, se_b
            for jc in range(NJC):
                nv = 4 - jc // 4
                pc += 1
                pe_m[jc] = pc
                vc += 2
                ve_m[jc] = vc
                sc_ += nv
                se_m[jc] = sc_

            @blk.tensor
            def _(e):
                if f1_prev:
                    wt(e, "se", f1_prev)
                for jc in range(NJC):
                    ic0 = jc // 4
                    base = (jc % 2) * 4
                    if jc >= 2:
                        wt(e, "se", se_m[jc - 2])
                    for ic in range(ic0, 4):
                        for c in range(EC):
                            last = e.matmul(
                                ps[base + ic][:],
                                kt[:, c * 2048 + jc * 128:c * 2048 + jc * 128 + 128],
                                qt[:, c * 2048 + ic * 512:c * 2048 + (ic + 1) * 512],
                                start=(c == 0), stop=(c == EC - 1))
                    inc(last, "pe")
            assert cnt["pe"] == pe_m[NJC - 1]

            @blk.vector
            def _(e):
                wt(e, "lc", mld + 16)
                for jc in range(NJC):
                    ic0 = jc // 4
                    base = (jc % 2) * 4
                    wt(e, "pe", pe_m[jc])
                    r = jc % 4
                    inc(e.tensor_add(ps[base + ic0][:], ps[base + ic0][:],
                                     fv(O_MASK + r * 512, 512)), "ve")
                    wt(e, "se", se_m[jc])
                    inc(e.reduce_sum(Zt[:, jc:jc + 1], ppseg(jc), axis=AX.X), "ve")
            assert cnt["ve"] == ve_m[NJC - 1]

            @blk.scalar
            def _(e):
                for jc in range(NJC):
                    ic0 = jc // 4
                    base = (jc % 2) * 4
                    wt(e, "pe", pe_m[jc])
                    for k_ in range(4 - ic0):
                        if k_ == 0:
                            wt(e, "ve", ve_m[jc] - 1)
                        inc(e.activation(
                            ppseg(jc)[:, k_ * 512:(k_ + 1) * 512],
                            ps[base + ic0 + k_][:], AF.Exp,
                            bias=0.0, scale=1.0 / 32.0), "se")
            assert cnt["se"] == se_m[NJC - 1]

            @blk.vector
            def _(e):
                chain(e, "ve")
                inc(e.reciprocal(RZ[:], Zt[:]), "ve")
                chain(e, "ve")
                for ts in range(16):
                    inc(e.tensor_scalar_mul(bfv(O_VV + ts * 512, 1024),
                                            bfv(O_VV + ts * 512, 1024),
                                            RZ[:, ts:ts + 1]), "ve")

            escope()
            scope(f"phase2_{seq}")
            barrier()

            pe_b2 = cnt["pe"]
            ve_b2 = cnt["ve"]
            xb2 = [cnt["lx0"], cnt["lx1"]]
            p2_pe = {es: pe_b2 + es + 1 for es in range(EC)}
            p2_ve = {es: ve_b2 + 4 * (es + 1) for es in range(EC)}
            xstr = [fv(O_XSTR0, 2048), fv(O_XSTR1, 2048)]

            @blk.sync
            def _(e):
                for es in range(EC):
                    if es >= 2:
                        wt(e, "ve", p2_ve[es - 2])
                    inc(e.dma_start(out=xstr[es % 2],
                                    in_=xt[seq, es * 128:(es + 1) * 128, :]),
                        "lx%d" % (es % 2), 16)

            @blk.tensor
            def _(e):
                for es in range(EC):
                    base = (es % 2) * 4
                    if es >= 2:
                        wt(e, "ve", p2_ve[es - 2])
                    for jc in range(NJC):
                        ic0 = jc // 4
                        for ic in range(ic0, 4):
                            last = e.matmul(
                                ps[base + ic][:],
                                bfv(O_VV + jc * 512, 1024)[:, es * 128:es * 128 + 128],
                                ppseg(jc)[:, (ic - ic0) * 512:(ic - ic0 + 1) * 512],
                                start=(jc == 0), stop=(jc == 4 * ic + 3))
                    inc(last, "pe")

            @blk.vector
            def _(e):
                if seq:
                    wt(e, *sp_marks[seq - 1])
                for es in range(EC):
                    base = (es % 2) * 4
                    wt(e, "pe", p2_pe[es])
                    wt(e, "lx%d" % (es % 2), xb2[es % 2] + 16 * (es // 2 + 1))
                    for ic in range(4):
                        inc(e.tensor_add(fv(O_RR + es * 2048 + ic * 512, 512),
                                         ps[base + ic][:],
                                         xstr[es % 2][:, ic * 512:(ic + 1) * 512]), "ve")
            assert cnt["ve"] == ve_b2 + 32

            escope()
            scope(f"ln1_{seq}")
            barrier()

            sqb = [O_SQ0, O_SQ1]
            se_b3 = cnt["se"]
            pe_b3 = cnt["pe"]
            ve_b3 = cnt["ve"]

            @blk.scalar
            def _(e):
                for c in range(EC):
                    if c >= 2:
                        wt(e, "pe", pe_b3 + c - 1)
                    inc(e.activation(bfv(sqb[c % 2] + 1024, 2048),
                                     fv(O_RR + c * 2048, 2048), AF.Square), "se")

            @blk.vector
            def _(e):
                for c in range(EC):
                    if c >= 2:
                        wt(e, "pe", pe_b3 + c - 1)
                    inc(e.tensor_copy(bfv(sqb[c % 2], 2048),
                                      fv(O_RR + c * 2048, 2048)), "ve")

            @blk.tensor
            def _(e):
                for c in range(EC):
                    wt(e, "se", se_b3 + c + 1)
                    wt(e, "ve", ve_b3 + c + 1)
                    for tc in range(TC):
                        e.matmul(ps[tc][:], ones_b[:],
                                 bfv(sqb[c % 2], 2048)[:, tc * 512:(tc + 1) * 512],
                                 start=(c == 0), stop=(c == EC - 1))
                        last = e.matmul(ps[4 + tc][:], ones_b[:],
                                 bfv(sqb[c % 2] + 1024, 2048)[:, tc * 512:(tc + 1) * 512],
                                 start=(c == 0), stop=(c == EC - 1))
                    inc(last, "pe")
            sqm["stats_pe"] = cnt["pe"]
            sqm["p2_pe"] = p2_pe[EC - 1]

            if seq + 1 < NSEQ:
                pf_base = cnt["ve"]
                pfl = [cnt["lx0"], cnt["lx1"]]
                pf_st = [fv(O_VV, 2048), fv(O_VV + 2048, 2048)]
                p2e = p2_pe[EC - 1]

                @blk.sync
                def _(e):
                    wt(e, "pe", p2e)
                    for c in range(EC):
                        if c >= 2:
                            wt(e, "ve", pf_base + c - 1)
                        inc(e.dma_start(out=pf_st[c % 2],
                                        in_=xt[seq + 1, c * 128:(c + 1) * 128, :]),
                            "lx%d" % (c % 2), 16)

                @blk.vector
                def _(e):
                    wt(e, "pe", p2e)
                    for c in range(EC):
                        wt(e, "lx%d" % (c % 2), pfl[c % 2] + 16 * (c // 2 + 1))
                        inc(e.tensor_copy(bfv(O_XTBF + c * 1024, 2048),
                                          pf_st[c % 2]), "ve")
                assert cnt["ve"] == pf_base + EC
                sqm["xpf"] = ("ve", pf_base + EC)

            barrier()

            e2o = 28672 if seq + 1 < NSEQ else O_E2B
            m2o = 30720 if seq + 1 < NSEQ else O_M2B

            @blk.scalar
            def _(e):
                for tc in range(TC):
                    inc(e.activation(muB[:, tc * 512:(tc + 1) * 512], ps[tc][:],
                                     AF.Copy, scale=1.0), "se")
                    inc(e.activation(fv(e2o + tc * 512, 512), ps[4 + tc][:],
                                     AF.Copy, bias=EPS, scale=1.0), "se")
                chain(e, "se")
                inc(e.activation(fv(m2o, 2048), muB[:, 0:2048], AF.Square), "se")
            f1 = cnt["se"]
            sqm["f1"] = f1

            @blk.vector
            def _(e):
                wt(e, "se", f1)
                inc(e.tensor_sub(fv(e2o, 2048), fv(e2o, 2048), fv(m2o, 2048)), "ve")
            f2 = cnt["ve"]

            @blk.scalar
            def _(e):
                wt(e, "ve", f2)
                inc(e.activation(fv(m2o, 2048), fv(e2o, 2048), AF.Ln), "se")
                chain(e, "se")
                inc(e.activation(rsB[:, 0:2048], fv(m2o, 2048), AF.Exp, scale=-0.5), "se")
            f3 = cnt["se"]
            sqm["f3"] = f3

            ve_b4 = cnt["ve"]
            gp_b4 = cnt["gp"]
            sqm["ve_apply_base"] = ve_b4

            @blk.vector
            def _(e):
                wt(e, "se", f3)
                for c in range(5):
                    inc(e.tensor_sub(fv(O_RR + c * 2048, 2048),
                                     fv(O_RR + c * 2048, 2048), muB[:, 0:2048]), "ve")
                    chain(e, "ve")
                    inc(e.tensor_mul(fv(O_RR + c * 2048, 2048),
                                     fv(O_RR + c * 2048, 2048), rsB[:, 0:2048]), "ve")
                    inc(e.drain(), "ve")

            @blk.gpsimd
            def _(e):
                wt(e, "se", f3)
                for c in range(5, EC):
                    inc(e.tensor_sub(fv(O_RR + c * 2048, 2048),
                                     fv(O_RR + c * 2048, 2048), muB[:, 0:2048]), "gp")
                    chain(e, "gp")
                    inc(e.tensor_mul(fv(O_RR + c * 2048, 2048),
                                     fv(O_RR + c * 2048, 2048), rsB[:, 0:2048]), "gp")
                    inc(e.drain(), "gp")
                for c in range(EC):
                    if c < 5:
                        wt(e, "ve", ve_b4 + 3 * (c + 1))
                    else:
                        wt(e, "gp", gp_b4 + 3 * (c - 5 + 1))
                    inc(e.dma_start(out=x1s.ap()[seq, c, :, :],
                                    in_=fv(O_RR + c * 2048, 2048)),
                        "sl%d" % seq, 16)
            sp_marks.append(("sl%d" % seq, cnt["sl%d" % seq]))

            escope()

        attn_end = dict(cnt)
        w_sw_marks = []
        _dqst = {"prev_dqve7": sqm["ve_apply_base"], "prev_lacse": 0,
                 "prev_lora_pe": 0}

        def dequant_w(w):
            prev_dqve = _dqst["prev_dqve7"]
            la_ld = cnt["gl"]

            @blk.sync
            def _(e):
                wt(e, "pe", sqm["p2_pe"])
                wt(e, "se", max(sqm["f3"], _dqst["prev_lacse"]))
                inc(e.dma_start(out=fv(O_LAST, 1024)[0:32, :], in_=la_h[w]), "gl", 16)
                inc(e.dma_start(out=fv(O_LBST, 1024)[0:32, :], in_=lb_h[w]), "gl", 16)

            @blk.scalar
            def _(e):
                wt(e, "gl", la_ld + 32)
                if _dqst["prev_lora_pe"]:
                    wt(e, "pe", _dqst["prev_lora_pe"])
                e.activation(bfv(O_LAR, 1024)[0:32, :],
                             fv(O_LAST, 1024)[0:32, :], AF.Copy)
                inc(e.activation(bfv(O_LBR, 1024)[0:32, :],
                                 fv(O_LBST, 1024)[0:32, :], AF.Copy), "se")
            lac_se = cnt["se"]
            _dqst["prev_lacse"] = lac_se

            qldb = [cnt["gq0"], cnt["gq1"], cnt["gq2"]]
            swb = [cnt["sw0"], cnt["sw1"]]
            se_bq = cnt["se"]
            pe_bq = cnt["pe"]
            veB = cnt["ve"]
            dq_ve = {c: veB + 3 * (c + 1) for c in range(EC)}

            @blk.sync
            def _(e):
                wt(e, "pe", sqm["p2_pe"])
                for c in range(EC):
                    wt(e, "ve", dq_ve[c - 3] if c >= 3 else prev_dqve)
                    inc(e.dma_start(out=iv(O_QI + (c % 3) * 1024, 1024),
                                    in_=qw_t[w, c * 128:(c + 1) * 128, :]),
                        "gq%d" % (c % 3), 16)
                    inc(e.dma_start(out=fv(O_SCL + (c % 3) * 1024, 1024),
                                    in_=sc_t[w, c * 128:(c + 1) * 128, :]),
                        "gq%d" % (c % 3), 16)

            @blk.gpsimd
            def _(e):
                for c in range(EC):
                    wt(e, "ve", dq_ve[c])
                    inc(e.dma_start(out=weffs.ap()[w, c],
                                    in_=bfv(O_WSTG + (c % 2) * 512, 1024)),
                        "sw%d" % (c % 2), 16)

            @blk.scalar
            def _(e):
                for c in range(EC):
                    wt(e, "gq%d" % (c % 3), qldb[c % 3] + 32 * (c // 3 + 1))
                    wt(e, "ve", dq_ve[c - 3] if c >= 3 else prev_dqve)
                    inc(e.activation(fv(O_TMP + (c % 3) * 1024, 1024),
                                     iv(O_QI + (c % 3) * 1024, 1024),
                                     AF.Copy, bias=-1.0, scale=1.0 / 7.5), "se")

            @blk.tensor
            def _(e):
                wt(e, "se", lac_se)
                wt(e, "ve", prev_dqve)
                for c in range(EC):
                    bank = 4 + 2 * (c % 2)
                    if c >= 2:
                        wt(e, "ve", dq_ve[c - 2])
                    e.matmul(ps[bank][:], bfv(O_LAR, 1024)[0:32, c * 128:(c + 1) * 128],
                             bfv(O_LBR, 1024)[0:32, 0:512], start=True, stop=True)
                    last = e.matmul(ps[bank + 1][:], bfv(O_LAR, 1024)[0:32, c * 128:(c + 1) * 128],
                             bfv(O_LBR, 1024)[0:32, 512:1024], start=True, stop=True)
                    inc(last, "pe")

            @blk.vector
            def _(e):
                for c in range(EC):
                    bank = 4 + 2 * (c % 2)
                    wt(e, "se", se_bq + c + 1)
                    wt(e, "gq%d" % (c % 3), qldb[c % 3] + 32 * (c // 3 + 1))
                    wt(e, "pe", pe_bq + c + 1)
                    wt(e, "sw%d" % (c % 2), swb[c % 2] + 16 * (c // 2))
                    inc(e.tensor_mul(fv(O_TMP + (c % 3) * 1024, 1024),
                                     fv(O_TMP + (c % 3) * 1024, 1024),
                                     fv(O_SCL + (c % 3) * 1024, 1024)), "ve")
                    chain(e, "ve")
                    e.tensor_add(bfv(O_WSTG + (c % 2) * 512, 1024)[:, 0:512],
                                 fv(O_TMP + (c % 3) * 1024, 512), ps[bank][:])
                    inc(e.tensor_add(bfv(O_WSTG + (c % 2) * 512, 1024)[:, 512:1024],
                                     fv(O_TMP + (c % 3) * 1024 + 512, 512),
                                     ps[bank + 1][:]), "ve")
                    inc(e.drain(), "ve")
            assert cnt["ve"] == dq_ve[EC - 1]
            _dqst["prev_dqve7"] = dq_ve[EC - 1]
            _dqst["prev_lora_pe"] = pe_bq + EC

            w_sw_marks.append((cnt["sw0"], cnt["sw1"]))
        WSL = [0, 4096]
        IBASE = [8192, 8192 + 14336]
        sqdb4 = sqd[:].bitcast(BF16)
        inst_free_pe = [attn_end["pe"], attn_end["pe"]]
        wslot_pe = [attn_end["pe"], attn_end["pe"]]
        stat_read = [("se", attn_end["se"]), ("se", attn_end["se"])]
        osb_sp = cnt["sp"]
        YB = [0, 1, 4, 5]
        ygrp = {"n": 0, "guard": {0: ("se", attn_end["se"]),
                                  1: ("se", attn_end["se"]),
                                  4: ("se", attn_end["se"]),
                                  5: ("se", attn_end["se"])}}
        tr_idx = {"n": 0}

        def bc512(tile_ap_tensor, col_off):
            return bass.AP(tile_ap_tensor, col_off, [[2048, 128], [0, 8], [1, 512]])

        def w_load(j):
            sl = j % 2
            nm = "lw%d" % sl
            free = wslot_pe[sl]
            swm = w_sw_marks[j]

            @blk.sync
            def _(e):
                wt(e, "pe", free)
                wt(e, "sw0", swm[0])
                wt(e, "sw1", swm[1])
                inc(e.dma_start(
                    out=bfv(WSL[sl], 8192).rearrange("p (c n) -> p c n", n=1024),
                    in_=weffs.ap()[j].rearrange("c p n -> p c n")), nm, 16)
            return (nm, cnt[nm])

        def x1_load(t, extra=()):
            i = t % 2
            seq_, off_ = t // (T // TT), (t % (T // TT)) * TT
            nm = "lf%d" % i
            free = inst_free_pe[i]

            @blk.sync
            def _(e):
                wt(e, "pe", free)
                for m in sp_marks:
                    wt(e, *m)
                for nm2, v2 in extra:
                    wt(e, nm2, v2)
                for c in range(EC):
                    inc(e.dma_start(out=fv(IBASE[i] + c * 512, 512),
                                    in_=x1s.ap()[seq_, c, :, off_:off_ + TT]), nm, 16)
            return (nm, cnt[nm])

        def cast_in(t, src_off, x1mark=None, extra=()):
            i = t % 2

            @blk.vector
            def _(e):
                if x1mark is not None:
                    wt(e, x1mark[0], x1mark[1])
                for nm2, v2 in extra:
                    wt(e, nm2, v2)
                chain(e, "ve")
                inc(e.tensor_copy(bfv(IBASE[i] + 12288, 4096), fv(src_off, 4096)), "ve")
            return ("ve", cnt["ve"])

        def y_mms(t, wj, wmark, castmark, evac_kind, evac_args):
            i = t % 2
            sl = wj % 2
            ioff = i * 2048
            pe_b_ = cnt["pe"]
            se_b_ = cnt["se"]
            ve_b_ = cnt["ve"]
            gp_b_ = cnt["gp"]
            guards = ygrp["guard"]
            g0 = ygrp["n"]

            b0, b1 = (2, 3) if i == 0 else (6, 7)
            rdmark = stat_read[i]

            def stat2(e, s_):
                if evac_kind == "y1":
                    src_sum = bfv(IBASE[i] + 4096, 4096)[:, s_ * 512:(s_ + 1) * 512]
                else:
                    src_sum = sqdb4[:, ioff + (s_ % 2) * 1024:ioff + (s_ % 2) * 1024 + 512]
                e.matmul(ps[b0][:], ones_b[:], src_sum,
                         start=(s_ == 0), stop=(s_ == EC - 1))
                return e.matmul(ps[b1][:], ones_b[:],
                                sqdb4[:, ioff + (s_ % 2) * 1024 + 512:ioff + (s_ % 2) * 1024 + 1024],
                                start=(s_ == 0), stop=(s_ == EC - 1))

            @blk.tensor
            def _(e):
                wt(e, castmark[0], castmark[1])
                wt(e, wmark[0], wmark[1])
                if evac_kind == "y1":
                    wt(e, rdmark[0], rdmark[1])
                for s in range(EC):
                    bank = YB[(g0 + s) % 4]
                    gg = guards.get(bank)
                    if gg is not None:
                        wt(e, gg[0], gg[1])
                    for c in range(EC):
                        src_mv = (bfv(IBASE[i] + 12288, 4096) if evac_kind == "y1"
                                  else bfv(IBASE[i] + 6144, 4096))
                        last = e.matmul(
                            ps[bank][:],
                            bfv(WSL[sl], 8192)[:, c * 1024 + s * 128:c * 1024 + (s + 1) * 128],
                            src_mv[:, c * 512:(c + 1) * 512],
                            start=(c == 0), stop=(c == EC - 1))
                    if evac_kind == "y1" and s >= 1:
                        wt(e, "se", se_b_ + 2 * s)
                        last = stat2(e, s - 1)
                    inc(last, "pe")
                    guards[bank] = (("se" if evac_kind == "y1" else "ve"),
                                    (se_b_ + 2 * s + 1) if evac_kind == "y1"
                                    else (ve_b_ + 2 * s + 1))
                if evac_kind == "y1":
                    wt(e, "se", se_b_ + 2 * EC)
                    inc(stat2(e, EC - 1), "pe")

            if evac_kind == "y1":
                ybf = bfv(IBASE[i] + 4096, 4096)

                @blk.scalar
                def _(e):
                    for s in range(EC):
                        wt(e, "pe", pe_b_ + s + 1)
                        inc(e.add(ybf[:, s * 512:(s + 1) * 512],
                                  ps[YB[(g0 + s) % 4]][:], evac_args[:, s:s + 1]), "se")
                        chain(e, "se")
                        inc(e.activation(
                            sqdb4[:, ioff + (s % 2) * 1024 + 512:ioff + (s % 2) * 1024 + 1024],
                            ybf[:, s * 512:(s + 1) * 512], AF.Square), "se")
            else:
                fbcol, in1_off, tail = evac_args

                @blk.vector
                def _(e):
                    for s in range(EC):
                        wt(e, "pe", pe_b_ + s + 1)
                        inc(e.scalar_tensor_tensor(
                            fv(IBASE[i] + 8192 + s * 512, 512),
                            ps[YB[(g0 + s) % 4]][:], fbcol[:, s:s + 1],
                            fv(in1_off + s * 512, 512), ALU.add, ALU.add), "ve")
                        chain(e, "ve")
                        if tail == "cast":
                            inc(e.tensor_copy(
                                bfv(IBASE[i] + 12288, 4096)[:, s * 512:(s + 1) * 512],
                                fv(IBASE[i] + 8192 + s * 512, 512)), "ve")
                        else:
                            inc(e.tensor_add(
                                fv(IBASE[i] + 8192 + s * 512, 512),
                                fv(IBASE[i] + 8192 + s * 512, 512),
                                fv(IBASE[i] + s * 512, 512)), "ve")
            ygrp["n"] += EC
            wslot_pe[sl] = cnt["pe"]
            return ("ve", cnt["ve"])

        def stats_mms(t, se_b_, ve_b_, sq_guard=True):
            i = t % 2
            b0, b1 = (2, 3) if i == 0 else (6, 7)
            ioff = i * 2048
            rdmark = stat_read_se[i]

            @blk.tensor
            def _(e):
                wt(e, "se", rdmark)
                for s_ in range(EC):
                    wt(e, "se", se_b_ + 2 * s_ + 2)
                    wt(e, "ve", ve_b_ + s_ + 1)
                    e.matmul(ps[b0][:], ones_b[:],
                             sqdb4[:, ioff + (s_ % 2) * 1024:ioff + (s_ % 2) * 1024 + 512],
                             start=(s_ == 0), stop=(s_ == EC - 1))
                    last = e.matmul(ps[b1][:], ones_b[:],
                             sqdb4[:, ioff + (s_ % 2) * 1024 + 512:ioff + (s_ % 2) * 1024 + 1024],
                             start=(s_ == 0), stop=(s_ == EC - 1))
                    inc(last, "pe")
            return cnt["pe"]

        def ln_stats_r2(t, src_off, wait_ve):
            i = t % 2
            ioff = i * 2048
            se_b_ = cnt["se"]
            ve_b_ = cnt["ve"]
            pe_b_ = cnt["pe"]

            @blk.scalar
            def _(e):
                wt(e, "ve", wait_ve)
                chain(e, "se")
                for s_ in range(EC):
                    if s_ >= 2:
                        wt(e, "pe", pe_b_ + s_ - 1)
                    inc(e.activation(
                        sqdb4[:, ioff + (s_ % 2) * 1024 + 512:ioff + (s_ % 2) * 1024 + 1024],
                        fv(src_off + s_ * 512, 512), AF.Square), "se")

            @blk.vector
            def _(e):
                chain(e, "ve")
                for s_ in range(EC):
                    if s_ >= 2:
                        wt(e, "pe", pe_b_ + s_ - 1)
                    inc(e.tensor_copy(
                        sqdb4[:, ioff + (s_ % 2) * 1024:ioff + (s_ % 2) * 1024 + 512],
                        fv(src_off + s_ * 512, 512)), "ve")

            b0, b1 = ((2, 3) if t % 2 == 0 else (6, 7))
            rdmark = stat_read[t % 2]

            @blk.tensor
            def _(e):
                wt(e, rdmark[0], rdmark[1])
                for s_ in range(EC):
                    wt(e, "se", se_b_ + s_ + 1)
                    wt(e, "ve", ve_b_ + s_ + 1)
                    e.matmul(ps[b0][:], ones_b[:],
                             sqdb4[:, ioff + (s_ % 2) * 1024:ioff + (s_ % 2) * 1024 + 512],
                             start=(s_ == 0), stop=(s_ == EC - 1))
                    last = e.matmul(ps[b1][:], ones_b[:],
                             sqdb4[:, ioff + (s_ % 2) * 1024 + 512:ioff + (s_ % 2) * 1024 + 1024],
                             start=(s_ == 0), stop=(s_ == EC - 1))
                    inc(last, "pe")
            return cnt["pe"]

        def ln_fin2(t, mu_copy=False):
            i = t % 2
            b0, b1 = (2, 3) if i == 0 else (6, 7)
            mcol = muB[:, i * 512:(i + 1) * 512]
            rcol = rsB[:, i * 512:(i + 1) * 512]
            ecol = e2s[:, i * 512:(i + 1) * 512]
            m2col = m2s[:, i * 512:(i + 1) * 512]
            p = cnt["pe"]

            @blk.scalar
            def _(e):
                wt(e, "pe", p)
                if mu_copy:
                    inc(e.copy(mcol, ps[b0][:]), "se")
                inc(e.activation(m2col, ps[b0][:], AF.Square), "se")
                inc(e.activation(ecol, ps[b1][:], AF.Copy, bias=EPS), "se")
            if mu_copy:
                stat_read[i] = ("se", cnt["se"])
            g1 = cnt["se"]

            @blk.vector
            def _(e):
                wt(e, "se", g1)
                inc(e.tensor_sub(ecol, ecol, m2col), "ve")
            g2 = cnt["ve"]

            @blk.scalar
            def _(e):
                wt(e, "ve", g2)
                inc(e.activation(m2col, ecol, AF.Ln), "se")
                chain(e, "se")
                inc(e.activation(rcol, m2col, AF.Exp, scale=-0.5), "se")
            return cnt["se"]

        def apply_gelu(t, blki, fin):
            i = t % 2
            b0 = 2 if i == 0 else 6
            ybf = bfv(IBASE[i] + 4096, 4096)
            y3 = ybf.rearrange("p (x b) -> p x b", b=512)
            mub = bass.AP(ps[b0], 0, [[512, 128], [0, 8], [1, 512]])

            @blk.vector
            def _(e):
                wt(e, "se", fin)
                chain(e, "ve")
                inc(e.tensor_sub(y3, y3, mub), "ve")
                chain(e, "ve")
                inc(e.tensor_mul(y3, y3, bc512(rsB, i * 512)), "ve")
            stat_read[i] = ("ve", cnt["ve"] - 1)
            am = cnt["ve"]

            @blk.scalar
            def _(e):
                wt(e, "ve", am)
                for s in range(EC):
                    inc(e.activation(bfv(IBASE[i] + 6144, 4096)[:, s * 512:(s + 1) * 512],
                                     ybf[:, s * 512:(s + 1) * 512], AF.Gelu,
                                     scale=gam[:, blki * 8 + s:blki * 8 + s + 1],
                                     bias=bet[:, blki * 8 + s:blki * 8 + s + 1]), "se")
            return ("se", cnt["se"])

        def r2_add(t):
            i = t % 2
            ho = IBASE[i] + 8192

            @blk.vector
            def _(e):
                chain(e, "ve")
                inc(e.tensor_add(fv(ho, 4096), fv(ho, 4096), fv(IBASE[i], 4096)), "ve")
            return cnt["ve"]

        def ln2_apply(t, fin):
            i = t % 2
            ho = IBASE[i] + 8192
            xo = IBASE[i]
            h3 = fv(ho, 4096).rearrange("p (x b) -> p x b", b=512)
            x3 = fv(xo, 4096).rearrange("p (x b) -> p x b", b=512)
            ve_o = cnt["ve"]

            @blk.vector
            def _(e):
                wt(e, "se", fin)
                chain(e, "ve")
                inc(e.tensor_sub(x3, h3, bc512(muB, i * 512)), "ve")
                chain(e, "ve")
                inc(e.tensor_mul(x3, x3, bc512(rsB, i * 512)), "ve")
            return cnt["ve"]

        def trans_store(t, ve_o):
            i = t % 2
            seq_, off_ = t // (T // TT), (t % (T // TT)) * TT
            xo = IBASE[i]
            pe_t = cnt["pe"]
            se_t = cnt["se"]
            sp_prev = globals_osb[0]

            @blk.tensor
            def _(e):
                wt(e, "ve", ve_o)
                idx = 0
                for tsl in range(4):
                    for c in range(EC):
                        n = tr_idx["n"] + idx
                        bank = 4 + n % 2
                        gg = ygrp["guard"].get(bank)
                        if gg is not None:
                            wt(e, gg[0], gg[1])
                        inc(e.transpose(ps[bank][:, 0:128],
                                        fv(xo + c * 512 + tsl * 128, 128), eye[:]),
                            "pe")
                        ygrp["guard"][bank] = ("se", se_t + idx + 1 + idx // EC)
                        idx += 1

            @blk.scalar
            def _(e):
                wt(e, "sp", sp_prev)
                idx = 0
                for tsl in range(4):
                    for c in range(EC):
                        n = tr_idx["n"] + idx
                        wt(e, "pe", pe_t + idx + 1)
                        inc(e.copy(fv(36864 + tsl * 1024 + c * 128, 128),
                                   ps[4 + n % 2][:, 0:128]), "se")
                        idx += 1
                    inc(e.drain(), "se")

            @blk.gpsimd
            def _(e):
                for tsl in range(4):
                    wt(e, "se", se_t + (tsl + 1) * (EC + 1))
                    inc(e.dma_start(
                        out=out_h[seq_, off_ + tsl * 128:off_ + (tsl + 1) * 128, :],
                        in_=fv(36864 + tsl * 1024, 1024)), "sp", 16)
            tr_idx["n"] += 32
            inst_free_pe[i] = cnt["pe"]
            return cnt["sp"]

        fbcols = [fb[:, w * 8:(w + 1) * 8] for w in range(6)]

        def emit_group(gname, ts, ex_x1=None, ex_cast=None, hook=None,
                       pre=None):
            nonlocal_osb = None
            scope(gname)
            xm = dict((t, m[0]) for t, m in (pre or {}).items())
            wl = {}
            cm = dict((t, m[1]) for t, m in (pre or {}).items())
            for j in (0, 1):
                if hook:
                    hook(j)
            if ts[0] not in xm:
                xm[ts[0]] = x1_load(ts[0], (ex_x1 or {}).get(ts[0], ()))
            wl[0] = w_load(0)
            wl[1] = w_load(1)
            for t in ts[1:]:
                if t not in xm:
                    xm[t] = x1_load(t, (ex_x1 or {}).get(t, ()))
            for t in ts:
                if t not in cm:
                    cm[t] = cast_in(t, IBASE[t % 2], xm[t],
                                    extra=(ex_cast or {}).get(t, ()))
            for blki in range(3):
                w1, w2 = 2 * blki, 2 * blki + 1
                gel = {}
                fin = {}
                for t in ts:
                    y_mms(t, w1, wl[w1], cm[t], "y1", fbcols[w1])
                    if t != ts[0]:
                        gel[ts[0]] = apply_gelu(ts[0], blki, fin[ts[0]])
                    fin[t] = ln_fin2(t)
                for t in ts:
                    if t not in gel:
                        gel[t] = apply_gelu(t, blki, fin[t])
                if w1 + 2 < 6:
                    if hook:
                        hook(w1 + 2)
                    wl[w1 + 2] = w_load(w1 + 2)
                for t in ts:
                    in1 = IBASE[t % 2] if blki == 0 else IBASE[t % 2] + 8192
                    tail = "cast" if blki < 2 else "r2"
                    cm[t] = y_mms(t, w2, wl[w2], gel[t], "y2",
                                  (fbcols[w2], in1, tail))
                if w2 + 2 < 6:
                    if hook:
                        hook(w2 + 2)
                    wl[w2 + 2] = w_load(w2 + 2)
            for t in ts:
                ln_stats_r2(t, IBASE[t % 2] + 8192, cm[t][1])
            vo = {}
            for t in ts:
                fin2 = ln_fin2(t, mu_copy=True)
                vo[t] = ln2_apply(t, fin2)
            for t in ts:
                globals_osb[0] = trans_store(t, vo[t])
            escope()

        globals_osb = [osb_sp]

        xm0 = x1_load(0, (("ve", attn_end["ve"]),))
        cm0 = cast_in(0, IBASE[0], xm0)

        scope("dequant")
        for w in range(6):
            dequant_w(w)
        escope()
        dq_end = dict(cnt)
        stat_read[1] = ("ve", dq_end["ve"])
        ygrp["guard"][4] = ("ve", dq_end["ve"])
        ygrp["guard"][5] = ("ve", dq_end["ve"])
        ex0 = {1: (("se", dq_end["se"]), ("ve", dq_end["ve"]),
                   ("gq0", dq_end["gq0"]), ("gq1", dq_end["gq1"]),
                   ("gq2", dq_end["gq2"]))}
        exc0 = {1: (("sw0", dq_end["sw0"]), ("sw1", dq_end["sw1"]))}
        emit_group("ffng0", (0, 1), ex_x1=ex0, ex_cast=exc0,
                   pre={0: (xm0, cm0)})
        emit_group("ffng1", (2, 3))
        emit_group("ffng2", (4, 5))
        emit_group("ffng3", (6, 7))

        barrier()

    return nc


_CACHE = {}


def host_shared(L0_attn_Wq, L0_attn_bq, L0_attn_Wk, L0_attn_bk, L0_attn_Wv,
                L0_attn_bv, L0_ffn_qweights, L0_ffn_scales, L0_ffn_biases,
                L0_ffn_lora_a, L0_ffn_lora_b, L0_ffn_ln_gamma, L0_ffn_ln_beta,
                **_ignored):
    wq_t = np.ascontiguousarray(np.asarray(L0_attn_Wq, np.float32).T)
    wk_t = np.ascontiguousarray(np.asarray(L0_attn_Wk, np.float32).T)
    wv_t = np.ascontiguousarray(np.asarray(L0_attn_Wv, np.float32).T)
    bq = np.asarray(L0_attn_bq, np.float32)
    bk = np.asarray(L0_attn_bk, np.float32)
    bv_ = np.asarray(L0_attn_bv, np.float32)
    bq_h = np.ascontiguousarray(bq.reshape(8, 128).T)
    bk_h = np.ascontiguousarray(bk.reshape(8, 128).T)
    bv_h = np.ascontiguousarray(np.broadcast_to(bv_[None, :], (128, E)))
    qw_t = np.ascontiguousarray(np.asarray(L0_ffn_qweights, np.int32).transpose(0, 2, 1))
    sc = np.asarray(L0_ffn_scales, np.float32)
    sc_t = np.ascontiguousarray(np.repeat(sc, 16, axis=2).transpose(0, 2, 1))
    la_h = np.ascontiguousarray(np.asarray(L0_ffn_lora_a, np.float32))
    lb_h = np.ascontiguousarray(np.asarray(L0_ffn_lora_b, np.float32).transpose(0, 2, 1))
    fb_h = np.ascontiguousarray(
        np.asarray(L0_ffn_biases, np.float32).reshape(6, 8, 128).transpose(2, 0, 1).reshape(128, 48))
    gam_h = np.ascontiguousarray(
        np.asarray(L0_ffn_ln_gamma, np.float32).reshape(3, 8, 128).transpose(2, 0, 1).reshape(128, 24))
    bet_h = np.ascontiguousarray(
        np.asarray(L0_ffn_ln_beta, np.float32).reshape(3, 8, 128).transpose(2, 0, 1).reshape(128, 24))
    p = np.arange(128)[:, None]
    c = np.arange(512)[None, :]
    mask_h = np.concatenate(
        [np.where(c >= p + 128 * r, 0.0, NEG).astype(np.float32) for r in range(4)],
        axis=1)
    ones_h = np.full((128, 128), 1.0 / 1024.0, np.float32)
    eye_h = np.eye(128, dtype=np.float32)

    return dict(wq_t=wq_t, wk_t=wk_t, wv_t=wv_t, bq_h=bq_h, bk_h=bk_h,
                bv_h=bv_h, qw_t=qw_t, sc_t=sc_t, la_h=la_h, lb_h=lb_h,
                fb_h=fb_h, gam_h=gam_h, bet_h=bet_h, mask_h=mask_h,
                ones_h=ones_h, eye_h=eye_h)


def kernel(**inputs):
    X = np.ascontiguousarray(np.asarray(inputs["L0_attn_X"], dtype=np.float32))
    shared = host_shared(**inputs)
    if "nc" not in _CACHE:
        _CACHE["nc"] = build_program()
    nc = _CACHE["nc"]
    in_maps = []
    for core in range(8):
        xtc = np.ascontiguousarray(X[2 * core:2 * core + 2].transpose(0, 2, 1))
        in_maps.append(dict(shared, xt=xtc))

    res = run_bass_kernel_spmd(nc, in_maps, core_ids=list(range(8)))
    out = np.concatenate([res.results[cr]["out"] for cr in range(8)], axis=0)
    return np.ascontiguousarray(out.astype(np.float32))
